# version 06
# baseline (speedup 1.0000x reference)
import contextlib
import numpy as np
import ml_dtypes
import concourse.bass as bass
import concourse.mybir as mybir
from concourse.bass_utils import run_bass_kernel_spmd

F32 = mybir.dt.float32
BF16 = mybir.dt.bfloat16
I32 = mybir.dt.int32
AF = mybir.ActivationFunctionType
ALU = mybir.AluOpType
AX = mybir.AxisListType

ENGS = ("sync", "scalar", "vector", "tensor", "gpsimd")


def real_cfg():
    return dict(T=2048, D=2048, I=8192, G=4, HD=64)


def mini_cfg():
    return dict(T=256, D=256, I=512, G=4, HD=64)


def derive(cfg):
    c = dict(cfg)
    c["TILE"] = c["T"] // 2
    c["NSUB"] = c["TILE"] // 128
    c["KC"] = c["D"] // 128
    c["ICT"] = c["I"] // 128
    c["ICH"] = c["ICT"] // 2
    c["D2"] = c["D"] // 2
    c["KVW"] = 2 * c["G"] * c["HD"]
    c["GH"] = c["D"] // c["HD"]
    c["HPG"] = c["GH"] // c["G"]
    return c



def host_prep(cfg, x, Wq, bq, Wkv, bkv, gw, gs, uw, us, dw, ds, n_cores):
    c = derive(cfg)
    T, D, KC, ICT, ICH, D2 = (c[k] for k in ("T", "D", "KC", "ICT", "ICH", "D2"))
    x = np.ascontiguousarray(np.asarray(x, dtype=np.float32).reshape(-1, D))
    assert x.shape[0] == T * n_cores

    def blk_gu(W):
        W4 = np.asarray(W).reshape(ICT, 128, KC, 128)
        return np.ascontiguousarray(W4.transpose(0, 3, 2, 1).reshape(
            ICT, 128, KC * 128))

    def blk_d(W):
        WT = np.ascontiguousarray(np.asarray(W).T)
        W4 = WT.reshape(2, ICH * 128, 4, D // 4)
        return np.ascontiguousarray(W4.transpose(0, 2, 1, 3))

    shared = dict(
        gwt=blk_gu(gw), uwt=blk_gu(uw), dwt=blk_d(dw),
        gs=np.ascontiguousarray(np.asarray(gs, np.float32).reshape(ICT, 128).T),
        us=np.ascontiguousarray(np.asarray(us, np.float32).reshape(ICT, 128).T),
        ds_r=np.ascontiguousarray(np.asarray(ds, np.float32).reshape(1, D)),
        bq_r=np.ascontiguousarray(np.asarray(bq, np.float32).reshape(1, D)),
        bkv_r=np.ascontiguousarray(np.asarray(bkv, np.float32).reshape(1, c["KVW"])),
        wq=np.ascontiguousarray(np.asarray(Wq, np.float32)),
        wkv=np.ascontiguousarray(np.asarray(Wkv, np.float32)),
        idn=np.eye(128, dtype=np.float32),
        idnb=np.eye(128, dtype=ml_dtypes.bfloat16))
    in_maps = []
    for cc in range(n_cores):
        m = dict(shared)
        m["x"] = np.ascontiguousarray(x[cc * T:(cc + 1) * T])
        in_maps.append(m)
    return in_maps



class Build:
    TOKENED = ("vector", "scalar", "gpsimd")

    def __init__(self, nc):
        self.nc = nc
        self.ops = {e: [] for e in ENGS}
        self.cnt = {}
        self.sem = {}
        self.aux = {}
        self.last_done = {}
        self.marks = []

    def mark(self, name):
        n = sum(1 for m, _ in self.marks if m.split("@")[0] == name)
        self.marks.append((f"{name}@{n}",
                           {e: len(self.ops[e]) for e in ENGS}))

    def sem_named(self, name):
        if name not in self.sem:
            self.sem[name] = self.stack.enter_context(self.nc.semaphore(name))
            self.cnt[name] = 0
        return self.sem[name]

    def n(self, name):
        return self.cnt.get(name, 0)

    def emit(self, eng, fn):
        self.ops[eng].append(fn)

    def wait(self, eng, name, val):
        if val is None or val <= 0:
            return
        s = self.sem_named(name)
        self.emit(eng, lambda E, s=s, v=val: E.wait_ge(s, v))

    def op(self, eng, fn, incs=(), dma=False):
        incs = list(incs)
        if eng in self.TOKENED and not dma:
            prev = self.last_done.get(eng)
            if prev is not None:
                self.wait(eng, prev[0], prev[1])
            if not incs:
                incs.append((eng[0] + "tok", 1))
        incs2 = []
        for name, k in incs:
            s = self.sem_named(name)
            self.cnt[name] = self.cnt.get(name, 0) + k
            incs2.append((s, k))
        if eng in self.TOKENED and not dma and incs:
            self.last_done[eng] = (incs[0][0], self.cnt[incs[0][0]])

        def run(E, fn=fn, incs2=tuple(incs2)):
            ins = fn(E)
            for s, k in incs2:
                ins.then_inc(s, k)
        self.emit(eng, run)

    def barrier(self):
        s = self.sem_named("bar")
        self.cnt["bar"] += len(ENGS)
        tgt = self.cnt["bar"]
        for e in ENGS:
            self.emit(e, lambda E, s=s: E.sem_inc(s, 1))
            self.emit(e, lambda E, s=s, v=tgt: E.wait_ge(s, v))

    def psum_fence(self, eng):
        for s in ("vfl", "vqb", "vkv", "vbc", "vte", "ssi", "vhh"):
            self.wait(eng, s, self.n(s))

    def bank_wait(self, eng, banks):
        rec = self.aux.setdefault("bank_rec", {})
        merged = {}
        for bk in banks:
            r = rec.get(bk)
            if r is not None:
                sname, val = r
                merged[sname] = max(merged.get(sname, 0), val)
        for sname, val in merged.items():
            self.wait(eng, sname, val)

    def bank_set(self, banks, sname, val):
        rec = self.aux.setdefault("bank_rec", {})
        for bk in banks:
            rec[bk] = (sname, val)


def build_program(cfg):
    c = derive(cfg)
    T, D, I, G, HD = (c[k] for k in ("T", "D", "I", "G", "HD"))
    TILE, NSUB, KC, ICT, ICH, D2, KVW, GH = (
        c[k] for k in ("TILE", "NSUB", "KC", "ICT", "ICH", "D2", "KVW", "GH"))
    HPG = c["HPG"]
    EPS = 1e-5
    NMMT = (TILE + 511) // 512
    ND2 = (D2 + 511) // 512
    D4 = D // 4
    NQ = (D + 511) // 512
    NKV = (KVW + 511) // 512
    NBAT = (KC + 3) // 4
    NLN = 1
    while D // NLN > 512 or D % NLN:
        NLN += 1
    NGRP = 2 * NLN
    RSZ = 8 + 4 * NGRP

    nc = bass.Bass("TRN2", target_bir_lowering=False)

    d_x = nc.dram_tensor("x", [T, D], F32, kind="ExternalInput").ap()
    d_wq = nc.dram_tensor("wq", [D, D], F32, kind="ExternalInput").ap()
    d_wkv = nc.dram_tensor("wkv", [D, KVW], F32, kind="ExternalInput").ap()
    d_bq = nc.dram_tensor("bq_r", [1, D], F32, kind="ExternalInput").ap()
    d_bkv = nc.dram_tensor("bkv_r", [1, KVW], F32, kind="ExternalInput").ap()
    d_gwt = nc.dram_tensor("gwt", [ICT, 128, KC * 128], I32, kind="ExternalInput").ap()
    d_uwt = nc.dram_tensor("uwt", [ICT, 128, KC * 128], I32, kind="ExternalInput").ap()
    d_dwt = nc.dram_tensor("dwt", [2, 4, ICH * 128, D4], I32, kind="ExternalInput").ap()
    d_gs = nc.dram_tensor("gs", [128, ICT], F32, kind="ExternalInput").ap()
    d_us = nc.dram_tensor("us", [128, ICT], F32, kind="ExternalInput").ap()
    d_dsr = nc.dram_tensor("ds_r", [1, D], F32, kind="ExternalInput").ap()
    d_idn = nc.dram_tensor("idn", [128, 128], F32, kind="ExternalInput").ap()
    d_idnb = nc.dram_tensor("idnb", [128, 128], BF16, kind="ExternalInput").ap()
    d_xo = nc.dram_tensor("xo", [T, D], F32, kind="ExternalOutput").ap()
    d_bc = nc.dram_tensor("bc", [3, 128, max(D, KVW)], F32, kind="Internal").ap()
    d_gwb = nc.dram_tensor("gwb", [ICT, 128, KC * 128], BF16, kind="Internal").ap()
    d_uwb = nc.dram_tensor("uwb", [ICT, 128, KC * 128], BF16, kind="Internal").ap()
    d_dwb = nc.dram_tensor("dwb", [2, 4, ICH * 128, D4], BF16, kind="Internal").ap()

    stack = contextlib.ExitStack()
    b = Build(nc)
    b.stack = stack

    with stack:
        e = stack.enter_context
        RB = ICH * TILE
        assert KC * KVW <= 8 * D
        WKVB2 = RB
        _curB = RB + KC * KVW
        _capB = RB + 8 * D
        _curA, _curC = 0, _capB
        if 2 * NSUB * KVW <= RB:
            _kvs_off = 0
            _curA = 2 * NSUB * KVW
        else:
            _kvs_off = _curC
            _curC += 2 * NSUB * KVW
        WQ_OFFS = {}
        for k in range(KC - 1, -1, -1):
            if _curB + D <= _capB:
                WQ_OFFS[k] = _curB
                _curB += D
            elif _curA + D <= RB:
                WQ_OFFS[k] = _curA
                _curA += D
            else:
                WQ_OFFS[k] = _curC
                _curC += D
        KVSF2 = _kvs_off // 2
        ARB_N = max(RB + 8 * D, _curC)
        ARB = e(nc.sbuf_tensor("ARB", [128, ARB_N], BF16))
        XS0, XS1 = 0, D
        Y0, Y1 = 2 * D, 3 * D
        DSBC = 4 * D
        TMP = 5 * D
        NTMP = max(3 * TILE, D + KVW + KVW + 128,
                   5 * NSUB * GH + D + 128)
        ARF_N = 5 * D + NTMP
        ARF = e(nc.sbuf_tensor("ARF", [128, ARF_N], F32))
        YT = e(nc.sbuf_tensor("YT", [128, KC * TILE], BF16))
        WG = e(nc.sbuf_tensor("WG", [128, 2 * KC * 128], BF16))
        WU = e(nc.sbuf_tensor("WU", [128, 2 * KC * 128], BF16))
        WD = e(nc.sbuf_tensor("WD", [128, 4 * D2], BF16))
        CON = e(nc.sbuf_tensor("CON", [128, 128 + 2 * ICT + 64], F32))
        ST = e(nc.sbuf_tensor("ST", [128, 2 * RSZ], F32))
        PS = e(nc.psum_tensor("PS", [128, 4096], F32))

        arb, arf = ARB[:], ARF[:]
        arbf = ARB[:].bitcast(F32)
        arbi = ARB[:].bitcast(I32)
        arfi = ARF[:].bitcast(I32)
        arfh = ARF[:].bitcast(BF16)
        psb = PS[:].bitcast(BF16)
        yt, wg, wu, wd, con, st, ps = (X[:] for X in (YT, WG, WU, WD, CON, ST, PS))
        yt3 = yt.rearrange("p (k q) -> p k q", q=TILE)

        idn_ap = con[:, 0:128]
        idnb_ap = con[:].bitcast(BF16)[:, 2 * (128 + 2 * ICT):
                                       2 * (128 + 2 * ICT) + 128]

        def gs_ap(icg):
            return con[:, 128 + icg:129 + icg]

        def us_ap(icg):
            return con[:, 128 + ICT + icg:129 + ICT + icg]

        IGOFF = RB // 2
        IUOFF = IGOFF + 2 * D

        def stx(ring, j):
            return st[:, ring * RSZ + j:ring * RSZ + j + 1]

        xo_flat = d_xo.rearrange("a b -> (a b)")
        x_flat = d_x.rearrange("a b -> (a b)")
        half_elems = T * D // 2
        b.op("sync", lambda E: E.dma_start(out=con[:, 0:128], in_=d_idn),
             incs=[("cld", 16)], dma=True)
        b.op("sync", lambda E: E.dma_start(
            out=con[:].bitcast(BF16)[:, 2 * (128 + 2 * ICT):
                                     2 * (128 + 2 * ICT) + 128],
            in_=d_idnb), incs=[("cld", 16)], dma=True)
        b.op("sync", lambda E: E.dma_start(out=con[:, 128:128 + ICT], in_=d_gs),
             incs=[("cld", 16)], dma=True)
        b.op("sync", lambda E: E.dma_start(
            out=con[:, 128 + ICT:128 + 2 * ICT], in_=d_us), incs=[("cld", 16)], dma=True)
        for j in range(2):
            b.op("sync", lambda E, j=j: E.dma_start(
                out=xo_flat[j * half_elems:(j + 1) * half_elems],
                in_=x_flat[j * half_elems:(j + 1) * half_elems]),
                incs=[("xinit", 16)], dma=True)

        def bcast_row(drow, width, ps_off, dst_off, scale, row_off=None,
                      extra_sync_waits=(), extra_vec_waits=()):
            if row_off is None:
                row_off = TMP
            b.wait("sync", "pbc", b.n("pbc"))
            for sl in range(3):
                b.wait("sync", f"gac{sl}", b.n(f"gac{sl}"))
            for sname, val in extra_sync_waits:
                b.wait("sync", sname, val)
            prevv = b.last_done.get("vector")
            if prevv is not None:
                b.wait("sync", prevv[0], prevv[1])
            b.op("sync", lambda E, row_off=row_off: E.dma_start(
                out=arf[0:1, row_off:row_off + width], in_=drow),
                incs=[("cld", 16)], dma=True)
            b.wait("vector", "cld", b.n("cld"))
            b.op("vector", lambda E, row_off=row_off: E.memset(
                arf[0:1, row_off + width:row_off + width + 128], 1.0),
                incs=[("von", 1)])
            b.wait("tensor", "cld", b.n("cld"))
            b.wait("tensor", "von", b.n("von"))
            b.bank_wait("tensor", range(ps_off // 512,
                                        (ps_off + width + 511) // 512))
            nmm = (width + 511) // 512
            for j in range(nmm):
                w = min(512, width - j * 512)
                b.op("tensor", lambda E, j=j, w=w, row_off=row_off: E.matmul(
                    ps[:, ps_off + j * 512:ps_off + j * 512 + w],
                    arf[0:1, row_off + width:row_off + width + 128],
                    arf[0:1, row_off + j * 512:row_off + j * 512 + w],
                    start=True, stop=True),
                    incs=([("pbc", 1)] if j == nmm - 1 else []))
            b.wait("vector", "pbc", b.n("pbc"))
            for sname, val in extra_vec_waits:
                b.wait("vector", sname, val)
            if scale is None:
                b.op("vector", lambda E: E.tensor_copy(
                    arf[:, dst_off:dst_off + width],
                    ps[:, ps_off:ps_off + width]), incs=[("vbc", 1)])
            else:
                b.op("vector", lambda E: E.tensor_scalar_mul(
                    arf[:, dst_off:dst_off + width],
                    ps[:, ps_off:ps_off + width], scale), incs=[("vbc", 1)])
            b.bank_set(range(ps_off // 512, (ps_off + width + 511) // 512),
                       "vbc", b.n("vbc"))

        bcast_row(d_bq, D, 0, DSBC, None)
        b.wait("sync", "vbc", b.n("vbc"))
        b.op("sync", lambda E: E.dma_start(out=d_bc[0][:, 0:D],
                                           in_=arf[:, DSBC:DSBC + D]),
             incs=[("bst", 16)], dma=True)
        bcast_row(d_bkv, KVW, 0, DSBC, None,
                  extra_sync_waits=[("bst", b.n("bst"))],
                  extra_vec_waits=[("bst", b.n("bst"))])
        b.wait("sync", "vbc", b.n("vbc"))
        b.op("sync", lambda E: E.dma_start(out=d_bc[2][:, 0:KVW],
                                           in_=arf[:, DSBC:DSBC + KVW]),
             incs=[("bst", 16)], dma=True)
        bcast_row(d_dsr, D, 0, DSBC, 0.5,
                  extra_sync_waits=[("bst", b.n("bst"))],
                  extra_vec_waits=[("bst", b.n("bst"))])
        b.wait("sync", "vbc", b.n("vbc"))
        b.op("sync", lambda E: E.dma_start(out=d_bc[1][:, 0:D],
                                           in_=arf[:, DSBC:DSBC + D]),
             incs=[("bst", 16)], dma=True)
        b.wait("gpsimd", "xinit", 32)
        b.barrier()

        def ybf(ts):
            return arfh[:, 4 * D + (ts % 4) * D:4 * D + (ts % 4) * D + D]

        def _stats_ts(tile, ts, base, prevln):
            ring = ts % 2
            xs = arf[:, (XS0 if ring == 0 else XS1):][:, 0:D]
            r0 = tile * TILE + ts * 128
            if ts >= 2:
                b.wait("scalar", "sy", base["sy"] + ts - 1)
                b.wait("scalar", "vrs", base["vrs"] + ts - 1)
            elif prevln is not None:
                b.wait("scalar", "sy", prevln["sy"])
                b.wait("scalar", "vrs", prevln["vrs"])
            b.op("scalar", lambda E, xs=xs, r0=r0: E.dma_start(
                out=xs, in_=d_xo[r0:r0 + 128, :]),
                incs=[(f"xld{ring}", 16)], dma=True)
            b.wait("vector", f"xld{ring}", b.n(f"xld{ring}"))
            wdt = D // NLN
            bn0 = ring * RSZ + 8
            for j in range(NLN):
                b.op("vector", lambda E, xs=xs, j=j, wdt=wdt, bn0=bn0:
                     E.bn_stats(out=st[:, bn0 + j * 6:bn0 + j * 6 + 6],
                                in_=xs[:, j * wdt:(j + 1) * wdt]),
                     incs=([("vrs", 1)] if j == NLN - 1 else []))
            six3 = st[:, bn0:bn0 + 3 * NGRP].rearrange("p (g c) -> p g c", c=3)
            m_ap = six3[:, :, 1]
            cv_ap = six3[:, :, 2]
            b.op("vector", lambda E, m_ap=m_ap, ring=ring: E.reduce_sum(
                stx(ring, 0), m_ap, axis=AX.X))
            b.op("vector", lambda E, cv_ap=cv_ap, ring=ring: E.reduce_sum(
                stx(ring, 1), cv_ap, axis=AX.X))
            b.op("vector", lambda E, m_ap=m_ap, ring=ring, bn0=bn0:
                 E.scalar_tensor_tensor(
                     out=st[:, bn0 + 3 * NGRP:bn0 + 4 * NGRP],
                     in0=m_ap, scalar=1.0, in1=m_ap,
                     op0=ALU.mult, op1=ALU.mult, accum_out=stx(ring, 2)))
            b.op("vector", lambda E, ring=ring: E.tensor_scalar_mul(
                stx(ring, 3), stx(ring, 0), 1.0 / NGRP))
            b.op("vector", lambda E, ring=ring: E.tensor_mul(
                stx(ring, 4), stx(ring, 3), stx(ring, 3)))
            b.op("vector", lambda E, ring=ring: E.scalar_tensor_tensor(
                out=stx(ring, 5), in0=stx(ring, 2), scalar=1.0 / NGRP,
                in1=stx(ring, 4), op0=ALU.mult, op1=ALU.subtract))
            b.op("vector", lambda E, ring=ring: E.scalar_tensor_tensor(
                out=stx(ring, 5), in0=stx(ring, 1), scalar=1.0 / D,
                in1=stx(ring, 5), op0=ALU.mult, op1=ALU.add))
            b.op("vector", lambda E, ring=ring: E.tensor_scalar_add(
                stx(ring, 5), stx(ring, 5), EPS), incs=[("vvar", 1)])
            b.wait("scalar", "vvar", b.n("vvar"))
            b.op("scalar", lambda E, ring=ring: E.activation(
                out=stx(ring, 6), in_=stx(ring, 5), func=AF.Sqrt),
                incs=[("sst", 1)])
            b.wait("vector", "sst", b.n("sst"))
            b.op("vector", lambda E, ring=ring: E.reciprocal(
                stx(ring, 7), stx(ring, 6)))
            b.op("vector", lambda E, ring=ring: E.tensor_scalar(
                out=stx(ring, 6), in0=stx(ring, 3), scalar1=stx(ring, 7),
                scalar2=-1.0, op0=ALU.mult, op1=ALU.mult), incs=[("vnb", 1)])
            b.wait("scalar", "vnb", b.n("vnb"))
            if ts >= 4:
                b.wait("scalar", "ptr", base["ptr"] + NBAT * (ts - 3))
            elif prevln is not None:
                b.wait("scalar", "ptr", prevln["ptr"] - NBAT * (3 - ts))
            b.op("scalar", lambda E, xs=xs, ts=ts, ring=ring: E.activation(
                out=ybf(ts), in_=xs, func=AF.Identity, bias=stx(ring, 6),
                scale=stx(ring, 7)), incs=[("sy", 1)])
            return b.n("sy")

        def _transp_ts(info, ts):
            b.wait("tensor", "sy", info["yrec"][ts])
            for bt in range(NBAT):
                js = list(range(bt * 4, min(KC, bt * 4 + 4)))
                gb = b.aux.get("tbat", 0)
                b.aux["tbat"] = gb + 1
                bank = gb % 8
                b.bank_wait("tensor", [bank])
                for jj, j in enumerate(js):
                    b.op("tensor", lambda E, j=j, jj=jj, bank=bank, ts=ts:
                         E.transpose(
                             psb[:, bank * 1024 + jj * 128:
                                 bank * 1024 + (jj + 1) * 128],
                             ybf(ts)[:, j * 128:(j + 1) * 128], idnb_ap),
                         incs=([("ptr", 1)] if j == js[-1] else []))
                b.wait("vector", "ptr", b.n("ptr"))
                nj = len(js)
                src = psb[:, bank * 1024:bank * 1024 + nj * 128].rearrange(
                    "p (j q) -> p j q", q=128)
                dst = yt3[:, bt * 4:bt * 4 + nj, ts * 128:ts * 128 + 128]
                b.op("vector", lambda E, src=src, dst=dst: E.tensor_copy(
                    dst, src), incs=[("vte", 1)])
                b.bank_set([bank], "vte", b.n("vte"))

        def _ln_entry_waits(extra_waits):
            for eng, sname, val in extra_waits:
                b.wait(eng, sname, val)
            prevv = b.last_done.get("vector")
            if prevv is not None:
                b.wait("scalar", prevv[0], prevv[1])
            for sl in range(3):
                b.wait("scalar", f"gac{sl}", b.n(f"gac{sl}"))
            for sl in range(2):
                b.wait("scalar", f"aac{sl}", b.n(f"aac{sl}"))

        def _ln_close(base):
            b.aux["ln_prev"] = dict(sy=b.n("sy"), vrs=b.n("vrs"),
                                    ptr=base["ptr"] + NBAT * NSUB)

        def ln_stats_junction(tile, extra_waits=()):
            prevln = b.aux.get("ln_prev")
            base = {"sy": b.n("sy"), "vrs": b.n("vrs"), "ptr": b.n("ptr")}
            _ln_entry_waits(extra_waits)
            info = dict(tile=tile, yrec=[], ptr_base=base["ptr"])
            nfirst = min(4, NSUB)
            for ts in range(nfirst):
                info["yrec"].append(_stats_ts(tile, ts, base, prevln))
            for ts in range(nfirst):
                _transp_ts(info, ts)
            for ts in range(nfirst, NSUB):
                info["yrec"].append(_stats_ts(tile, ts, base, prevln))
            info["done"] = nfirst
            _ln_close(base)
            return info

        def ln_transposes(info):
            for ts in range(info.get("done", 0), NSUB):
                _transp_ts(info, ts)
            return b.n("vte")

        def ln_full(tile, extra_waits=()):
            prevln = b.aux.get("ln_prev")
            base = {"sy": b.n("sy"), "vrs": b.n("vrs"), "ptr": b.n("ptr")}
            _ln_entry_waits(extra_waits)
            info = dict(tile=tile, yrec=[], ptr_base=base["ptr"])
            for ts in range(NSUB):
                info["yrec"].append(_stats_ts(tile, ts, base, prevln))
                _transp_ts(info, ts)
            _ln_close(base)
            return b.n("vte")

        def ln_junction(tile):
            def j():
                extra = [("scalar", "vhh", b.n("vhh"))]
                info = ln_stats_junction(tile, extra_waits=extra)
                b.aux["pending_ln"] = info
            return j

        def take_pending_ln(tile):
            pend = b.aux.pop("pending_ln", None)
            if pend is not None:
                assert pend["tile"] == tile
                return ln_transposes(pend)
            return None

        def ffn_phase(tile, is_int, junction=None):
            b.mark("ln")
            b.wait("vector", "bld", b.n("bld"))
            yT_ready = take_pending_ln(tile)
            if yT_ready is None:
                yT_ready = ln_full(tile)
            ln_ptr_final = b.n("ptr")

            for half in range(2):
                b.mark(f"gu{half}")
                hh_base = b.n("vhh")
                flush_base_prev_half = b.n("vfl")
                for ic in range(ICH):
                    icg = half * ICH + ic
                    par = ic % 2
                    pg = ps[:, par * 2048:par * 2048 + TILE]
                    pu = ps[:, par * 2048 + 1024:par * 2048 + 1024 + TILE]
                    for stream in ("g", "u"):
                        dint = d_gwt if stream == "g" else d_uwt
                        dscr = d_gwb if stream == "g" else d_uwb
                        wring = wg if stream == "g" else wu
                        nslab = b.n(stream + "slab")
                        b.cnt[stream + "slab"] = nslab + 1
                        slot = nslab % 2
                        soff = slot * KC * 128
                        pe_done = (lambda n: 2 * n + 1) if stream == "g" else \
                            (lambda n: 2 * n + 2)
                        wbrec = b.aux.setdefault(stream + "wbrec", {})
                        if is_int:
                            ioff = (IGOFF if stream == "g" else IUOFF) + slot * D
                            if nslab >= 2:
                                b.wait("sync", stream + "dq", nslab - 1)
                            b.op("sync", lambda E, dint=dint, icg=icg, ioff=ioff:
                                 E.dma_start(out=arbi[:, ioff:ioff + D],
                                             in_=dint[icg]),
                                 incs=[(f"{stream}il{slot}", 16)], dma=True)
                            if nslab >= 2:
                                b.wait("scalar", "pgu", pe_done(nslab - 2))
                                wname, wval = wbrec[nslab - 2]
                                b.wait("scalar", wname, wval)
                            b.wait("scalar", f"{stream}il{slot}",
                                   b.n(f"{stream}il{slot}"))
                            b.op("scalar", lambda E, ioff=ioff, wring=wring,
                                 soff=soff: E.activation(
                                     out=wring[:, soff:soff + KC * 128],
                                     in_=arbi[:, ioff:ioff + D],
                                     func=AF.Copy, bias=-128.0),
                                 incs=[(stream + "dq", 1)])
                            b.wait("gpsimd", stream + "dq", nslab + 1)
                            b.op("gpsimd", lambda E, dscr=dscr, icg=icg,
                                 wring=wring, soff=soff: E.dma_start(
                                     out=dscr[icg],
                                     in_=wring[:, soff:soff + KC * 128]),
                                 incs=[(f"{stream}wb{slot}", 16)], dma=True)
                            wbrec[nslab] = (f"{stream}wb{slot}",
                                            b.n(f"{stream}wb{slot}"))
                            b.wait("tensor", stream + "dq", nslab + 1)
                        else:
                            if nslab >= 2:
                                b.wait("sync", "pgu", pe_done(nslab - 2))
                            b.op("sync", lambda E, dscr=dscr, icg=icg,
                                 wring=wring, soff=soff: E.dma_start(
                                     out=wring[:, soff:soff + KC * 128],
                                     in_=dscr[icg]),
                                 incs=[(f"{stream}ld{slot}", 16)], dma=True)
                            b.wait("tensor", f"{stream}ld{slot}",
                                   b.n(f"{stream}ld{slot}"))
                        dst = pg if stream == "g" else pu
                        if stream == "g":
                            if icg == 0:
                                b.wait("tensor", "vte", yT_ready)
                            b.bank_wait("tensor",
                                        range(par * 4, par * 4 + 2 * NMMT))
                        for k in range(KC):
                            for j in range(NMMT):
                                w = min(512, TILE - j * 512)
                                last = (k == KC - 1 and j == NMMT - 1)
                                b.op("tensor", lambda E, k=k, j=j, w=w, dst=dst,
                                     wring=wring, soff=soff: E.matmul(
                                         dst[:, j * 512:j * 512 + w],
                                         wring[:, soff + k * 128:
                                               soff + (k + 1) * 128],
                                         yt[:, k * TILE + j * 512:
                                            k * TILE + j * 512 + w],
                                         start=(k == 0), stop=(k == KC - 1)),
                                     incs=([("pgu", 1)] if last else []))
                    sg = arf[:, (Y0 if par == 0 else Y1):][:, 0:TILE]
                    if ic < 2:
                        b.wait("scalar", "ptr", ln_ptr_final)
                    else:
                        b.wait("scalar", "vhh", hh_base + ic - 1)
                    b.wait("scalar", "pgu", b.n("pgu") - 1)
                    b.op("scalar", lambda E, pg=pg, sg=sg, icg=icg: E.activation(
                        out=sg, in_=pg, func=AF.Silu, scale=gs_ap(icg)),
                        incs=[("ssi", 1)])
                    b.bank_set(range(par * 4, par * 4 + NMMT), "ssi",
                               b.n("ssi"))
                    b.wait("vector", "pgu", b.n("pgu"))
                    b.wait("vector", "ssi", b.n("ssi"))
                    if ic == 0:
                        b.wait("vector", "dpe", b.aux.get("half_dpe", 0))
                        b.wait("vector", "pq", b.n("pq"))
                    b.op("vector", lambda E, sg=sg, pu=pu, ic=ic, icg=icg:
                         E.scalar_tensor_tensor(
                             out=arb[:, ic * TILE:(ic + 1) * TILE],
                             in0=sg, scalar=us_ap(icg), in1=pu,
                             op0=ALU.mult, op1=ALU.mult),
                         incs=[("vhh", 1)])
                    b.bank_set(range(par * 4 + NMMT, par * 4 + 2 * NMMT),
                               "vhh", b.n("vhh"))
                hh_done = b.n("vhh")

                if half == 1 and junction is not None:
                    b.mark("junc")
                    junction()

                b.mark(f"dn{half}")
                ntq = (NSUB + 3) // 4
                nch_per = max(1, ICH // 2)
                if is_int:
                    b.wait("sync", "vhh", hh_done)
                dstart_chunk = b.n("dch")
                for tq in range(ntq):
                    tss = list(range(tq * 4, min(NSUB, tq * 4 + 4)))
                    for hq in range(4):
                        gidx = b.aux.get("dgrp", 0)
                        b.aux["dgrp"] = gidx + 1
                        bbase = 4 * (gidx % 2)
                        grp_pdn_base = b.n("pdn")
                        for ch in range(nch_per):
                            ics = ([2 * ch, 2 * ch + 1] if ICH >= 2 else [0])
                            nsl = len(ics)
                            nch = b.n("dch")
                            b.cnt["dch"] = nch + 1
                            slot = nch % 4
                            soff = slot * 2 * D4
                            src_i = d_dwt[half, hq].rearrange(
                                "(cc p) h -> p cc h", p=128)[
                                :, ics[0]:ics[0] + nsl, :]
                            src_b = d_dwb[half, hq].rearrange(
                                "(cc p) h -> p cc h", p=128)[
                                :, ics[0]:ics[0] + nsl, :]
                            dstw = wd[:, soff:soff + nsl * D4].rearrange(
                                "p (cc h) -> p cc h", h=D4)
                            chunkwb = b.aux.setdefault("dch_wb", {})
                            if is_int and tq == 0:
                                idch = b.n("idch")
                                b.cnt["idch"] = idch + 1
                                islot = idch % 2
                                ioff = Y0 + islot * 2 * D4
                                if idch >= 2:
                                    b.wait("sync", "ddq", idch - 1)
                                b.op("sync", lambda E, src_i=src_i, ioff=ioff,
                                     nsl=nsl: E.dma_start(
                                         out=arfi[:, ioff:ioff + nsl * D4]
                                         .rearrange("p (cc h) -> p cc h", h=D4),
                                         in_=src_i),
                                     incs=[(f"dil{islot}", 16)], dma=True)
                                if nch >= 4:
                                    b.wait("scalar", "dpe", nch - 3)
                                    wbprev = chunkwb.get(nch - 4)
                                    if wbprev is not None:
                                        b.wait("scalar", wbprev[0], wbprev[1])
                                b.wait("scalar", f"dil{islot}",
                                       b.n(f"dil{islot}"))
                                b.op("scalar", lambda E, ioff=ioff, soff=soff,
                                     nsl=nsl: E.activation(
                                         out=wd[:, soff:soff + nsl * D4],
                                         in_=arfi[:, ioff:ioff + nsl * D4],
                                         func=AF.Copy, bias=-128.0),
                                     incs=[("ddq", 1)])
                                b.wait("gpsimd", "ddq", b.n("ddq"))
                                b.op("gpsimd", lambda E, src_b=src_b, dstw=dstw:
                                     E.dma_start(out=src_b, in_=dstw),
                                     incs=[(f"dwb{slot}", 16)], dma=True)
                                chunkwb[nch] = (f"dwb{slot}", b.n(f"dwb{slot}"))
                                b.wait("tensor", "ddq", b.n("ddq"))
                            else:
                                if is_int and tq == 1 and hq == 0 and ch == 0:
                                    for sl in range(4):
                                        b.wait("sync", f"dwb{sl}",
                                               b.n(f"dwb{sl}"))
                                if nch >= 4:
                                    b.wait("sync", "dpe", nch - 3)
                                    wbprev = chunkwb.get(nch - 4)
                                    if wbprev is not None:
                                        b.wait("sync", wbprev[0], wbprev[1])
                                chunkwb[nch] = None
                                b.op("sync", lambda E, src_b=src_b, dstw=dstw:
                                     E.dma_start(out=dstw, in_=src_b),
                                     incs=[(f"dld{slot}", 16)], dma=True)
                                b.wait("tensor", f"dld{slot}",
                                       b.n(f"dld{slot}"))
                            if nch == dstart_chunk:
                                b.wait("tensor", "vhh", hh_done)
                            for jj, ic in enumerate(ics):
                                for tsi, tsv in enumerate(tss):
                                    bk = bbase + tsi
                                    psd = ps[:, bk * 512:bk * 512 + D4]
                                    if ic == 0:
                                        b.bank_wait("tensor", [bk])
                                    lastmm = (ic == ICH - 1)
                                    lastofchunk = (jj == nsl - 1 and
                                                   tsi == len(tss) - 1)
                                    incs = []
                                    if lastofchunk:
                                        incs.append(("dpe", 1))
                                    elif lastmm and tsi < len(tss) - 1:
                                        incs.append(("pdn", 1))
                                    b.op("tensor", lambda E, psd=psd, soff=soff,
                                         jj=jj, tsv=tsv, ic=ic: E.matmul(
                                             psd,
                                             arb[:, ic * TILE + tsv * 128:
                                                 ic * TILE + tsv * 128 + 128],
                                             wd[:, soff + jj * D4:
                                                soff + (jj + 1) * D4],
                                             start=(ic == 0),
                                             stop=(ic == ICH - 1)),
                                         incs=incs)
                        for tsi, tsv in enumerate(tss):
                            bk = bbase + tsi
                            psd = ps[:, bk * 512:bk * 512 + D4]
                            nfl = b.n("vfl")
                            tslot = nfl % 3
                            toff = TMP + tslot * TILE
                            if tsi < len(tss) - 1:
                                b.wait("vector", "pdn", grp_pdn_base + tsi + 1)
                            else:
                                b.wait("vector", "dpe", b.n("dch"))
                            b.wait("vector", f"gac{tslot}",
                                   b.n(f"gac{tslot}"))
                            b.op("vector", lambda E, psd=psd, toff=toff,
                                 hq=hq: E.tensor_tensor(
                                     out=arf[:, toff:toff + D4], in0=psd,
                                     in1=arf[:, DSBC + hq * D4:
                                             DSBC + hq * D4 + D4],
                                     op=ALU.mult),
                                 incs=[("vfl", 1)])
                            b.bank_set([bk], "vfl", b.n("vfl"))
                            r0 = tile * TILE + tsv * 128
                            b.wait("gpsimd", "vfl", b.n("vfl"))
                            accrec = b.aux.setdefault("accrec", {})
                            prevacc = accrec.get((r0, hq))
                            if prevacc is not None:
                                b.wait("gpsimd", prevacc[0], prevacc[1])
                            b.op("gpsimd", lambda E, toff=toff, r0=r0, hq=hq:
                                 E.dma_start(
                                     out=d_xo[r0:r0 + 128,
                                              hq * D4:hq * D4 + D4],
                                     in_=arf[:, toff:toff + D4],
                                     accum_op=ALU.add),
                                 incs=[(f"gac{tslot}", 16)], dma=True)
                            accrec[(r0, hq)] = (f"gac{tslot}",
                                                b.n(f"gac{tslot}"))
                b.aux["half_dpe"] = b.n("dch")

        def wq_off(k):
            return WQ_OFFS[k]

        def preload_B():
            for s in ("gdq", "udq", "ddq"):
                b.wait("gpsimd", s, b.n(s))
            for s in ("gil0", "gil1", "uil0", "uil1", "dil0", "dil1"):
                b.wait("gpsimd", s, b.n(s))
            for k in range(KC):
                b.op("gpsimd", lambda E, k=k: E.dma_start(
                    out=arb[:, WKVB2 + k * KVW:WKVB2 + (k + 1) * KVW],
                    in_=d_wkv[k * 128:(k + 1) * 128, :]),
                    incs=[("qlb", 16)], dma=True)
            for k in range(KC):
                if WQ_OFFS[k] >= RB:
                    b.op("gpsimd", lambda E, k=k: E.dma_start(
                        out=arb[:, wq_off(k):wq_off(k) + D],
                        in_=d_wq[k * 128:(k + 1) * 128, :]),
                        incs=[("qlb", 16)], dma=True)

        def load_A():
            b.wait("gpsimd", "dpe", b.n("dpe"))
            b.wait("gpsimd", "vhh", b.n("vhh"))
            for k in range(KC):
                if WQ_OFFS[k] < RB:
                    b.op("gpsimd", lambda E, k=k: E.dma_start(
                        out=arb[:, wq_off(k):wq_off(k) + D],
                        in_=d_wq[k * 128:(k + 1) * 128, :]),
                        incs=[("qla", 16)], dma=True)

        def attn_phase(tile, first, junction=None):
            BKVB = TMP + D + KVW
            if first:
                load_A()
                b.wait("sync", "vfl", b.n("vfl"))
                b.wait("sync", "bst", b.n("bst"))
                b.op("sync", lambda E: E.dma_start(
                    out=arf[:, DSBC:DSBC + D], in_=d_bc[0][:, 0:D]),
                    incs=[("bld", 16)], dma=True)
                for sl in range(3):
                    b.wait("sync", f"gac{sl}", b.n(f"gac{sl}"))
                b.op("sync", lambda E: E.dma_start(
                    out=arf[:, BKVB:BKVB + KVW], in_=d_bc[2][:, 0:KVW]),
                    incs=[("bld", 16)], dma=True)
            b.mark("aln")
            yT_ready = take_pending_ln(tile)
            if yT_ready is None:
                yT_ready = ln_full(tile)

            b.mark("akv")
            kv_base = b.n("vkv")
            b.wait("vector", "dpe", b.n("dpe"))
            b.wait("vector", "bld", b.n("bld"))
            for ts in range(NSUB):
                par = ts % 2
                pkv = ps[:, par * 1024:par * 1024 + KVW]
                if ts == 0:
                    b.wait("tensor", "vte", yT_ready)
                    b.wait("tensor", "qlb", b.n("qlb"))
                b.bank_wait("tensor",
                            range(par * 2, par * 2 + (KVW + 511) // 512))
                for k in range(KC):
                    for j in range(NKV):
                        w = min(512, KVW - j * 512)
                        last = (k == KC - 1 and j == NKV - 1)
                        b.op("tensor", lambda E, k=k, j=j, w=w, pkv=pkv, ts=ts:
                             E.matmul(
                                 pkv[:, j * 512:j * 512 + w],
                                 yt3[:, k, ts * 128:ts * 128 + 128],
                                 arb[:, WKVB2 + k * KVW + j * 512:
                                     WKVB2 + k * KVW + j * 512 + w],
                                 start=(k == 0), stop=(k == KC - 1)),
                             incs=([("pkv", 1)] if last else []))
                b.wait("vector", "pkv", b.n("pkv"))
                b.op("vector", lambda E, pkv=pkv, ts=ts: E.tensor_add(
                    arbf[:, KVSF2 + ts * KVW:KVSF2 + (ts + 1) * KVW], pkv,
                    arf[:, BKVB:BKVB + KVW]), incs=[("vkv", 1)])
                b.bank_set(range(par * 2, par * 2 + (KVW + 511) // 512),
                           "vkv", b.n("vkv"))

            b.mark("aq")
            qb_base = b.n("vqb")
            b.wait("vector", "sy", b.n("sy"))
            for sl in range(2):
                b.wait("vector", f"aac{sl}", b.n(f"aac{sl}"))
            for ts in range(NSUB):
                par = ts % 2
                pq = ps[:, par * 2048:par * 2048 + D]
                if ts == 0:
                    b.wait("tensor", "qla", b.n("qla"))
                b.bank_wait("tensor",
                            range(par * 4, par * 4 + (D + 511) // 512))
                for k in range(KC):
                    for j in range(NQ):
                        w = min(512, D - j * 512)
                        last = (k == KC - 1 and j == NQ - 1)
                        b.op("tensor", lambda E, k=k, j=j, w=w, pq=pq, ts=ts:
                             E.matmul(
                                 pq[:, j * 512:j * 512 + w],
                                 yt3[:, k, ts * 128:ts * 128 + 128],
                                 arb[:, wq_off(k) + j * 512:
                                     wq_off(k) + j * 512 + w],
                                 start=(k == 0), stop=(k == KC - 1)),
                             incs=([("pq", 1)] if last else []))
                b.wait("vector", "pq", b.n("pq"))
                if ts >= 2:
                    b.wait("vector", "gpr", b.n("gpr") - 1)
                b.op("vector", lambda E, pq=pq: E.tensor_add(
                    arf[:, XS0:XS0 + D], pq, arf[:, DSBC:DSBC + D]),
                    incs=[("vqb", 1)])
                b.bank_set(range(par * 4, par * 4 + (D + 511) // 512),
                           "vqb", b.n("vqb"))
                kv0 = KVSF2 + ts * KVW
                kview = arbf[:, kv0:kv0 + KVW].rearrange(
                    "p (g c d) -> p g c d", g=G, c=2)[:, :, 0, :] \
                    .unsqueeze(2).broadcast_to([128, G, HPG, HD])
                qview = arf[:, XS0:XS0 + D].rearrange(
                    "p (g h d) -> p g h d", g=G, h=HPG)
                if ts == 0:
                    b.wait("gpsimd", "vkv", b.n("vkv"))
                b.wait("gpsimd", "vqb", b.n("vqb"))
                b.wait("gpsimd", "gsr", b.n("gsr"))
                b.op("gpsimd", lambda E, kview=kview, qview=qview:
                     E.tensor_tensor(
                         out=arf[:, XS1:XS1 + D].rearrange(
                             "p (g h d) -> p g h d", g=G, h=HPG),
                         in0=qview, in1=kview, op=ALU.mult),
                     incs=[("gpr", 1)])
                b.wait("vector", "gpr", b.n("gpr"))
                b.op("vector", lambda E, ts=ts: E.reduce_sum(
                    arf[:, TMP + ts * GH:TMP + (ts + 1) * GH],
                    arf[:, XS1:XS1 + D].rearrange("p (s d) -> p s d", d=HD),
                    axis=AX.X), incs=[("gsr", 1)])

            if junction is not None:
                b.mark("junc")
                junction()

            SALL = TMP
            S8 = TMP + NSUB * GH
            D0 = S8 + NSUB * GH
            EE = D0 + NSUB * GH
            ATT = EE + NSUB * GH
            nsc = NSUB * GH
            b.wait("vector", "gsr", b.n("gsr"))
            b.op("vector", lambda E: E.tensor_scalar_mul(
                arf[:, S8:S8 + nsc], arf[:, SALL:SALL + nsc], 1.0 / 8.0))
            b.op("vector", lambda E: E.tensor_sub(
                arf[:, D0:D0 + nsc], arf[:, S8:S8 + nsc],
                arf[:, S8:S8 + nsc]), incs=[("vd0", 1)])
            b.wait("scalar", "vd0", b.n("vd0"))
            b.op("scalar", lambda E: E.activation(
                out=arf[:, EE:EE + nsc], in_=arf[:, D0:D0 + nsc], func=AF.Exp),
                incs=[("sxp", 1)])
            b.wait("vector", "sxp", b.n("sxp"))
            b.op("vector", lambda E: E.reciprocal(
                arf[:, D0:D0 + nsc], arf[:, EE:EE + nsc]))
            b.op("vector", lambda E: E.tensor_mul(
                arf[:, ATT:ATT + nsc], arf[:, EE:EE + nsc],
                arf[:, D0:D0 + nsc]))
            b.mark("aoa")
            b.wait("vector", "sy", b.n("sy"))
            for ts in range(NSUB):
                oslot = XS1 if ts % 2 else XS0
                b.wait("vector", f"aac{ts % 2}", b.n(f"aac{ts % 2}"))
                kv0 = KVSF2 + ts * KVW
                aview = arf[:, ATT + ts * GH:ATT + (ts + 1) * GH].rearrange(
                    "p (g h) -> p g h", g=G).unsqueeze(3) \
                    .broadcast_to([128, G, HPG, HD])
                vview = arbf[:, kv0:kv0 + KVW].rearrange(
                    "p (g c d) -> p g c d", g=G, c=2)[:, :, 1, :] \
                    .unsqueeze(2).broadcast_to([128, G, HPG, HD])
                b.op("vector", lambda E, aview=aview, vview=vview, oslot=oslot:
                     E.tensor_tensor(
                         out=arf[:, oslot:oslot + D].rearrange(
                             "p (g h d) -> p g h d", g=G, h=HPG),
                         in0=aview, in1=vview, op=ALU.mult),
                     incs=[("voa", 1)])
                r0 = tile * TILE + ts * 128
                b.wait("gpsimd", "voa", b.n("voa"))
                accrec = b.aux.setdefault("accrec", {})
                for hq in range(4):
                    prevacc = accrec.get((r0, hq))
                    if prevacc is not None:
                        b.wait("gpsimd", prevacc[0], prevacc[1])
                b.op("gpsimd", lambda E, oslot=oslot, r0=r0, ts=ts:
                     E.dma_start(
                         out=d_xo[r0:r0 + 128, :], in_=arf[:, oslot:oslot + D],
                         accum_op=ALU.add), incs=[(f"aac{ts % 2}", 16)],
                     dma=True)
                for hq in range(4):
                    accrec[(r0, hq)] = (f"aac{ts % 2}", b.n(f"aac{ts % 2}"))

        b.wait("scalar", "xinit", 32)
        b.wait("scalar", "cld", b.n("cld"))
        b.wait("vector", "vbc", b.n("vbc"))
        b.mark("f1a")
        ffn_phase(0, is_int=True)
        for s in ("gwb0", "gwb1", "uwb0", "uwb1",
                  "dwb0", "dwb1", "dwb2", "dwb3"):
            b.wait("sync", s, b.n(s))
        b.mark("f1b")
        preload_B()
        ffn_phase(1, is_int=False, junction=ln_junction(0))
        b.mark("att0")
        attn_phase(0, first=True, junction=ln_junction(1))
        b.mark("att1")

        def f2a_junction():
            b.wait("sync", "vqb", b.n("vqb"))
            b.op("sync", lambda E: E.dma_start(
                out=arf[:, DSBC:DSBC + D], in_=d_bc[1][:, 0:D]),
                incs=[("bld", 16)], dma=True)
            ln_junction(0)()

        attn_phase(1, first=False, junction=f2a_junction)
        b.mark("f2a")
        ffn_phase(0, is_int=False, junction=ln_junction(1))
        b.mark("f2b")
        ffn_phase(1, is_int=False)
        b.mark("tail")
        for sl in range(3):
            b.wait("gpsimd", f"gac{sl}", b.n(f"gac{sl}"))
        for sl in range(2):
            b.wait("gpsimd", f"aac{sl}", b.n(f"aac{sl}"))

        segs = []
        prev = ("init", {e: 0 for e in ENGS})
        for name, pos in b.marks + [("end", {e: len(b.ops[e]) for e in ENGS})]:
            segs.append((prev[0], prev[1], pos))
            prev = (name, pos)
        segs.append((prev[0], prev[1], {e: len(b.ops[e]) for e in ENGS}))

        with nc.Block() as block:
            for eng in ENGS:
                def body(E, eng=eng):
                    idx = 0
                    for sname, p0, p1 in segs:
                        lo, hi = p0[eng], p1[eng]
                        if hi > lo:
                            with nc.named_scope(sname):
                                for fn in b.ops[eng][lo:hi]:
                                    fn(E)
                getattr(block, eng)(body)

    return nc



def kernel(L0_attn_x, L0_attn_Wq, L0_attn_bq, L0_attn_Wkv, L0_attn_bkv,
           L0_ffn_gate_w, L0_ffn_gate_scale, L0_ffn_up_w, L0_ffn_up_scale,
           L0_ffn_down_w, L0_ffn_down_scale):
    cfg = real_cfg()
    c = derive(cfg)
    n_cores = 8
    in_maps = host_prep(cfg, L0_attn_x, L0_attn_Wq, L0_attn_bq, L0_attn_Wkv,
                        L0_attn_bkv, L0_ffn_gate_w, L0_ffn_gate_scale,
                        L0_ffn_up_w, L0_ffn_up_scale, L0_ffn_down_w,
                        L0_ffn_down_scale, n_cores)
    nc = build_program(cfg)
    res = run_bass_kernel_spmd(nc, in_maps, core_ids=list(range(n_cores)))
    out = np.concatenate([np.asarray(r["xo"], dtype=np.float32)
                          for r in res.results], axis=0)
    return out.reshape(1, c["T"] * n_cores, c["D"])


# version 07 (final)
# speedup vs baseline: 1.1225x; 1.1225x over previous
import contextlib
import numpy as np
import ml_dtypes
import concourse.bass as bass
import concourse.mybir as mybir
from concourse.bass_utils import run_bass_kernel_spmd

F32 = mybir.dt.float32
BF16 = mybir.dt.bfloat16
I32 = mybir.dt.int32
AF = mybir.ActivationFunctionType
ALU = mybir.AluOpType
AX = mybir.AxisListType

ENGS = ("sync", "scalar", "vector", "tensor", "gpsimd")


def real_cfg():
    return dict(T=2048, D=2048, I=8192, G=4, HD=64)


def mini_cfg():
    return dict(T=256, D=256, I=512, G=4, HD=64)


def derive(cfg):
    c = dict(cfg)
    c["TILE"] = c["T"] // 2
    c["NSUB"] = c["TILE"] // 128
    c["KC"] = c["D"] // 128
    c["ICT"] = c["I"] // 128
    c["ICH"] = c["ICT"] // 2
    c["D2"] = c["D"] // 2
    c["KVW"] = 2 * c["G"] * c["HD"]
    c["GH"] = c["D"] // c["HD"]
    c["HPG"] = c["GH"] // c["G"]
    assert c["D"] == 2 * c["TILE"]
    return c



def host_prep(cfg, x, Wq, bq, Wkv, bkv, gw, gs, uw, us, dw, ds, n_cores):
    c = derive(cfg)
    T, D, KC, ICT, ICH, D2 = (c[k] for k in ("T", "D", "KC", "ICT", "ICH", "D2"))
    x = np.ascontiguousarray(np.asarray(x, dtype=np.float32).reshape(-1, D))
    assert x.shape[0] == T * n_cores

    def blk_gu(W):
        W4 = np.asarray(W).reshape(ICT, 128, KC, 128)
        return np.ascontiguousarray(W4.transpose(0, 3, 2, 1).reshape(
            ICT, 128, KC * 128))

    def blk_d(W):
        WT = np.ascontiguousarray(np.asarray(W).T)
        W4 = WT.reshape(2, ICH * 128, 4, D // 4)
        return np.ascontiguousarray(W4.transpose(0, 2, 1, 3))

    shared = dict(
        gwt=blk_gu(gw), uwt=blk_gu(uw), dwt=blk_d(dw),
        gs=np.ascontiguousarray(np.asarray(gs, np.float32).reshape(ICT, 128).T),
        us=np.ascontiguousarray(np.asarray(us, np.float32).reshape(ICT, 128).T),
        ds_r=np.ascontiguousarray(np.asarray(ds, np.float32).reshape(1, D)),
        bq_r=np.ascontiguousarray(np.asarray(bq, np.float32).reshape(1, D)),
        bkv_r=np.ascontiguousarray(np.asarray(bkv, np.float32).reshape(1, c["KVW"])),
        wq=np.ascontiguousarray(np.asarray(Wq, np.float32)),
        wkv=np.ascontiguousarray(np.asarray(Wkv, np.float32)),
        idn=np.eye(128, dtype=np.float32),
        idnb=np.eye(128, dtype=ml_dtypes.bfloat16))
    in_maps = []
    for cc in range(n_cores):
        m = dict(shared)
        m["x"] = np.ascontiguousarray(x[cc * T:(cc + 1) * T])
        in_maps.append(m)
    return in_maps



class Build:
    TOKENED = ("vector", "scalar", "gpsimd")

    def __init__(self, nc):
        self.nc = nc
        self.ops = {e: [] for e in ENGS}
        self.cnt = {}
        self.sem = {}
        self.aux = {}
        self.last_done = {}
        self.marks = []

    def mark(self, name):
        n = sum(1 for m, _ in self.marks if m.split("@")[0] == name)
        self.marks.append((f"{name}@{n}",
                           {e: len(self.ops[e]) for e in ENGS}))

    def sem_named(self, name):
        if name not in self.sem:
            self.sem[name] = self.stack.enter_context(self.nc.semaphore(name))
            self.cnt[name] = 0
        return self.sem[name]

    def n(self, name):
        return self.cnt.get(name, 0)

    def emit(self, eng, fn):
        self.ops[eng].append(fn)

    def wait(self, eng, name, val):
        if val is None or val <= 0:
            return
        s = self.sem_named(name)
        self.emit(eng, lambda E, s=s, v=val: E.wait_ge(s, v))

    def op(self, eng, fn, incs=(), dma=False):
        incs = list(incs)
        if eng in self.TOKENED and not dma:
            prev = self.last_done.get(eng)
            if prev is not None:
                self.wait(eng, prev[0], prev[1])
            if not incs:
                incs.append((eng[0] + "tok", 1))
        incs2 = []
        for name, k in incs:
            s = self.sem_named(name)
            self.cnt[name] = self.cnt.get(name, 0) + k
            incs2.append((s, k))
        if eng in self.TOKENED and not dma and incs:
            self.last_done[eng] = (incs[0][0], self.cnt[incs[0][0]])

        def run(E, fn=fn, incs2=tuple(incs2)):
            ins = fn(E)
            for s, k in incs2:
                ins.then_inc(s, k)
        self.emit(eng, run)

    def barrier(self):
        s = self.sem_named("bar")
        self.cnt["bar"] += len(ENGS)
        tgt = self.cnt["bar"]
        for e in ENGS:
            self.emit(e, lambda E, s=s: E.sem_inc(s, 1))
            self.emit(e, lambda E, s=s, v=tgt: E.wait_ge(s, v))

    def psum_fence(self, eng):
        for s in ("vfl", "vqb", "vkv", "vbc", "vte", "ssi", "vhh"):
            self.wait(eng, s, self.n(s))

    def bank_wait(self, eng, banks):
        rec = self.aux.setdefault("bank_rec", {})
        merged = {}
        for bk in banks:
            r = rec.get(bk)
            if r is not None:
                sname, val = r
                merged[sname] = max(merged.get(sname, 0), val)
        for sname, val in merged.items():
            self.wait(eng, sname, val)

    def bank_set(self, banks, sname, val):
        rec = self.aux.setdefault("bank_rec", {})
        for bk in banks:
            rec[bk] = (sname, val)


def build_program(cfg):
    c = derive(cfg)
    T, D, I, G, HD = (c[k] for k in ("T", "D", "I", "G", "HD"))
    TILE, NSUB, KC, ICT, ICH, D2, KVW, GH = (
        c[k] for k in ("TILE", "NSUB", "KC", "ICT", "ICH", "D2", "KVW", "GH"))
    HPG = c["HPG"]
    EPS = 1e-5
    NMMT = (TILE + 511) // 512
    ND2 = (D2 + 511) // 512
    D4 = D // 4
    NQ = (D + 511) // 512
    NKV = (KVW + 511) // 512
    NBAT = (KC + 3) // 4
    NLN = 1
    while D // NLN > 512 or D % NLN:
        NLN += 1
    NGRP = 2 * NLN
    RSZ = 8 + 4 * NGRP

    nc = bass.Bass("TRN2", target_bir_lowering=False)

    d_x = nc.dram_tensor("x", [T, D], F32, kind="ExternalInput").ap()
    d_wq = nc.dram_tensor("wq", [D, D], F32, kind="ExternalInput").ap()
    d_wkv = nc.dram_tensor("wkv", [D, KVW], F32, kind="ExternalInput").ap()
    d_bq = nc.dram_tensor("bq_r", [1, D], F32, kind="ExternalInput").ap()
    d_bkv = nc.dram_tensor("bkv_r", [1, KVW], F32, kind="ExternalInput").ap()
    d_gwt = nc.dram_tensor("gwt", [ICT, 128, KC * 128], I32, kind="ExternalInput").ap()
    d_uwt = nc.dram_tensor("uwt", [ICT, 128, KC * 128], I32, kind="ExternalInput").ap()
    d_dwt = nc.dram_tensor("dwt", [2, 4, ICH * 128, D4], I32, kind="ExternalInput").ap()
    d_gs = nc.dram_tensor("gs", [128, ICT], F32, kind="ExternalInput").ap()
    d_us = nc.dram_tensor("us", [128, ICT], F32, kind="ExternalInput").ap()
    d_dsr = nc.dram_tensor("ds_r", [1, D], F32, kind="ExternalInput").ap()
    d_idn = nc.dram_tensor("idn", [128, 128], F32, kind="ExternalInput").ap()
    d_idnb = nc.dram_tensor("idnb", [128, 128], BF16, kind="ExternalInput").ap()
    d_xo = nc.dram_tensor("xo", [T, D], F32, kind="ExternalOutput").ap()
    d_bc = nc.dram_tensor("bc", [3, 128, max(D, KVW)], F32, kind="Internal").ap()
    d_gwb = nc.dram_tensor("gwb", [ICT, 128, KC * 128], BF16, kind="Internal").ap()
    d_uwb = nc.dram_tensor("uwb", [ICT, 128, KC * 128], BF16, kind="Internal").ap()
    d_dwb = nc.dram_tensor("dwb", [2, 4, ICH * 128, D4], BF16, kind="Internal").ap()

    stack = contextlib.ExitStack()
    b = Build(nc)
    b.stack = stack

    with stack:
        e = stack.enter_context
        RB = ICH * TILE
        assert KC * KVW <= 8 * D
        WKVB2 = RB
        _curB = RB + KC * KVW
        _capB = RB + 8 * D
        _curA, _curC = 0, _capB
        if 2 * NSUB * KVW <= RB:
            _kvs_off = 0
            _curA = 2 * NSUB * KVW
        else:
            _kvs_off = _curC
            _curC += 2 * NSUB * KVW
        WQ_OFFS = {}
        for k in range(KC - 1, -1, -1):
            if _curB + D <= _capB:
                WQ_OFFS[k] = _curB
                _curB += D
            elif _curA + D <= RB:
                WQ_OFFS[k] = _curA
                _curA += D
            else:
                WQ_OFFS[k] = _curC
                _curC += D
        KVSF2 = _kvs_off // 2
        ARB_N = max(RB + 8 * D, _curC)
        ARB = e(nc.sbuf_tensor("ARB", [128, ARB_N], BF16))
        XS0, XS1 = 0, D
        Y0, Y1 = 2 * D, 3 * D
        DSBC = 4 * D
        TMP = 5 * D
        NTMP = max(3 * TILE, D + KVW + KVW + 128,
                   5 * NSUB * GH + D + 128)
        ARF_N = 5 * D + NTMP
        ARF = e(nc.sbuf_tensor("ARF", [128, ARF_N], F32))
        YT = e(nc.sbuf_tensor("YT", [128, KC * TILE], BF16))
        WG = e(nc.sbuf_tensor("WG", [128, 2 * KC * 128], BF16))
        WU = e(nc.sbuf_tensor("WU", [128, 2 * KC * 128], BF16))
        WD = e(nc.sbuf_tensor("WD", [128, 4 * D2], BF16))
        CON = e(nc.sbuf_tensor("CON", [128, 128 + 2 * ICT + 64], F32))
        ST = e(nc.sbuf_tensor("ST", [128, 2 * RSZ], F32))
        PS = e(nc.psum_tensor("PS", [128, 4096], F32))

        arb, arf = ARB[:], ARF[:]
        arbf = ARB[:].bitcast(F32)
        arbi = ARB[:].bitcast(I32)
        arfi = ARF[:].bitcast(I32)
        arfh = ARF[:].bitcast(BF16)
        psb = PS[:].bitcast(BF16)
        yt, wg, wu, wd, con, st, ps = (X[:] for X in (YT, WG, WU, WD, CON, ST, PS))
        yt3 = yt.rearrange("p (k q) -> p k q", q=TILE)

        idn_ap = con[:, 0:128]
        idnb_ap = con[:].bitcast(BF16)[:, 2 * (128 + 2 * ICT):
                                       2 * (128 + 2 * ICT) + 128]

        def gs_ap(icg):
            return con[:, 128 + icg:129 + icg]

        def us_ap(icg):
            return con[:, 128 + ICT + icg:129 + ICT + icg]

        IGOFF = RB // 2
        IUOFF = IGOFF + 2 * D

        def stx(ring, j):
            return st[:, ring * RSZ + j:ring * RSZ + j + 1]

        xo_flat = d_xo.rearrange("a b -> (a b)")
        x_flat = d_x.rearrange("a b -> (a b)")
        half_elems = T * D // 2
        b.op("sync", lambda E: E.dma_start(out=con[:, 0:128], in_=d_idn),
             incs=[("cld", 16)], dma=True)
        b.op("sync", lambda E: E.dma_start(
            out=con[:].bitcast(BF16)[:, 2 * (128 + 2 * ICT):
                                     2 * (128 + 2 * ICT) + 128],
            in_=d_idnb), incs=[("cld", 16)], dma=True)
        b.op("sync", lambda E: E.dma_start(out=con[:, 128:128 + ICT], in_=d_gs),
             incs=[("cld", 16)], dma=True)
        b.op("sync", lambda E: E.dma_start(
            out=con[:, 128 + ICT:128 + 2 * ICT], in_=d_us), incs=[("cld", 16)], dma=True)
        for j in range(2):
            b.op("gpsimd", lambda E, j=j: E.dma_start(
                out=xo_flat[j * half_elems:(j + 1) * half_elems],
                in_=x_flat[j * half_elems:(j + 1) * half_elems]),
                incs=[("xinit", 16)], dma=True)

        def bcast_row(drow, width, ps_off, dst_off, scale, row_off=None,
                      extra_sync_waits=(), extra_vec_waits=()):
            if row_off is None:
                row_off = TMP
            b.wait("sync", "pbc", b.n("pbc"))
            for sl in range(3):
                b.wait("sync", f"gac{sl}", b.n(f"gac{sl}"))
            for sname, val in extra_sync_waits:
                b.wait("sync", sname, val)
            prevv = b.last_done.get("vector")
            if prevv is not None:
                b.wait("sync", prevv[0], prevv[1])
            b.op("sync", lambda E, row_off=row_off: E.dma_start(
                out=arf[0:1, row_off:row_off + width], in_=drow),
                incs=[("cld", 16)], dma=True)
            b.wait("vector", "cld", b.n("cld"))
            b.op("vector", lambda E, row_off=row_off: E.memset(
                arf[0:1, row_off + width:row_off + width + 128], 1.0),
                incs=[("von", 1)])
            b.wait("tensor", "cld", b.n("cld"))
            b.wait("tensor", "von", b.n("von"))
            b.bank_wait("tensor", range(ps_off // 512,
                                        (ps_off + width + 511) // 512))
            nmm = (width + 511) // 512
            for j in range(nmm):
                w = min(512, width - j * 512)
                b.op("tensor", lambda E, j=j, w=w, row_off=row_off: E.matmul(
                    ps[:, ps_off + j * 512:ps_off + j * 512 + w],
                    arf[0:1, row_off + width:row_off + width + 128],
                    arf[0:1, row_off + j * 512:row_off + j * 512 + w],
                    start=True, stop=True),
                    incs=([("pbc", 1)] if j == nmm - 1 else []))
            b.wait("vector", "pbc", b.n("pbc"))
            for sname, val in extra_vec_waits:
                b.wait("vector", sname, val)
            if scale is None:
                b.op("vector", lambda E: E.tensor_copy(
                    arf[:, dst_off:dst_off + width],
                    ps[:, ps_off:ps_off + width]), incs=[("vbc", 1)])
            else:
                b.op("vector", lambda E: E.tensor_scalar_mul(
                    arf[:, dst_off:dst_off + width],
                    ps[:, ps_off:ps_off + width], scale), incs=[("vbc", 1)])
            b.bank_set(range(ps_off // 512, (ps_off + width + 511) // 512),
                       "vbc", b.n("vbc"))

        bcast_row(d_bq, D, 0, DSBC, None)
        b.wait("sync", "vbc", b.n("vbc"))
        b.op("sync", lambda E: E.dma_start(out=d_bc[0][:, 0:D],
                                           in_=arf[:, DSBC:DSBC + D]),
             incs=[("bst", 16)], dma=True)
        bcast_row(d_bkv, KVW, 0, DSBC, None,
                  extra_sync_waits=[("bst", b.n("bst"))],
                  extra_vec_waits=[("bst", b.n("bst"))])
        b.wait("sync", "vbc", b.n("vbc"))
        b.op("sync", lambda E: E.dma_start(out=d_bc[2][:, 0:KVW],
                                           in_=arf[:, DSBC:DSBC + KVW]),
             incs=[("bst", 16)], dma=True)
        bcast_row(d_dsr, D, 0, DSBC, 0.5,
                  extra_sync_waits=[("bst", b.n("bst"))],
                  extra_vec_waits=[("bst", b.n("bst"))])
        b.wait("sync", "vbc", b.n("vbc"))
        b.op("sync", lambda E: E.dma_start(out=d_bc[1][:, 0:D],
                                           in_=arf[:, DSBC:DSBC + D]),
             incs=[("bst", 16)], dma=True)
        b.wait("gpsimd", "xinit", 32)
        b.barrier()

        def ybf(ts, ring=4):
            j = ts % ring
            return arfh[:, 4 * D + j * D:4 * D + (j + 1) * D]

        def _stats_ts(tile, ts, base, prevln):
            ring = ts % 2
            xs = arf[:, (XS0 if ring == 0 else XS1):][:, 0:D]
            r0 = tile * TILE + ts * 128
            if ts >= 2:
                b.wait("scalar", "sy", base["sy"] + ts - 1)
                b.wait("scalar", "vrs", base["vrs"] + ts - 1)
            elif prevln is not None:
                b.wait("scalar", "sy", prevln["sy"])
                b.wait("scalar", "vrs", prevln["vrs"])
            b.op("scalar", lambda E, xs=xs, r0=r0: E.dma_start(
                out=xs, in_=d_xo[r0:r0 + 128, :]),
                incs=[(f"xld{ring}", 16)], dma=True)
            b.wait("vector", f"xld{ring}", b.n(f"xld{ring}"))
            wdt = D // NLN
            bn0 = ring * RSZ + 8
            for j in range(NLN):
                b.op("vector", lambda E, xs=xs, j=j, wdt=wdt, bn0=bn0:
                     E.bn_stats(out=st[:, bn0 + j * 6:bn0 + j * 6 + 6],
                                in_=xs[:, j * wdt:(j + 1) * wdt]),
                     incs=([("vrs", 1)] if j == NLN - 1 else []))
            six3 = st[:, bn0:bn0 + 3 * NGRP].rearrange("p (g c) -> p g c", c=3)
            m_ap = six3[:, :, 1]
            cv_ap = six3[:, :, 2]
            b.op("vector", lambda E, m_ap=m_ap, ring=ring: E.reduce_sum(
                stx(ring, 0), m_ap, axis=AX.X))
            b.op("vector", lambda E, cv_ap=cv_ap, ring=ring: E.reduce_sum(
                stx(ring, 1), cv_ap, axis=AX.X))
            b.op("vector", lambda E, m_ap=m_ap, ring=ring, bn0=bn0:
                 E.scalar_tensor_tensor(
                     out=st[:, bn0 + 3 * NGRP:bn0 + 4 * NGRP],
                     in0=m_ap, scalar=1.0, in1=m_ap,
                     op0=ALU.mult, op1=ALU.mult, accum_out=stx(ring, 2)))
            b.op("vector", lambda E, ring=ring: E.tensor_scalar_mul(
                stx(ring, 3), stx(ring, 0), 1.0 / NGRP))
            b.op("vector", lambda E, ring=ring: E.tensor_mul(
                stx(ring, 4), stx(ring, 3), stx(ring, 3)))
            b.op("vector", lambda E, ring=ring: E.scalar_tensor_tensor(
                out=stx(ring, 5), in0=stx(ring, 2), scalar=1.0 / NGRP,
                in1=stx(ring, 4), op0=ALU.mult, op1=ALU.subtract))
            b.op("vector", lambda E, ring=ring: E.scalar_tensor_tensor(
                out=stx(ring, 5), in0=stx(ring, 1), scalar=1.0 / D,
                in1=stx(ring, 5), op0=ALU.mult, op1=ALU.add))
            b.op("vector", lambda E, ring=ring: E.tensor_scalar_add(
                stx(ring, 5), stx(ring, 5), EPS), incs=[("vvar", 1)])
            b.wait("scalar", "vvar", b.n("vvar"))
            b.op("scalar", lambda E, ring=ring: E.activation(
                out=stx(ring, 6), in_=stx(ring, 5), func=AF.Sqrt),
                incs=[("sst", 1)])
            b.wait("vector", "sst", b.n("sst"))
            b.op("vector", lambda E, ring=ring: E.reciprocal(
                stx(ring, 7), stx(ring, 6)))
            b.op("vector", lambda E, ring=ring: E.tensor_scalar(
                out=stx(ring, 6), in0=stx(ring, 3), scalar1=stx(ring, 7),
                scalar2=-1.0, op0=ALU.mult, op1=ALU.mult), incs=[("vnb", 1)])
            b.wait("scalar", "vnb", b.n("vnb"))
            yring = b.aux.get("yring", 4)
            if ts >= yring:
                b.wait("scalar", "ptr", base["ptr"] + NBAT * (ts - yring + 1))
            elif prevln is not None:
                b.wait("scalar", "ptr", prevln["ptr"] - NBAT * (yring - 1 - ts))
            yring = b.aux.get("yring", 4)
            b.op("scalar", lambda E, xs=xs, ts=ts, ring=ring, yring=yring:
                 E.activation(
                     out=ybf(ts, yring), in_=xs, func=AF.Identity,
                     bias=stx(ring, 6), scale=stx(ring, 7)), incs=[("sy", 1)])
            return b.n("sy")

        def _transp_ts(info, ts):
            yring = info.get("yring", 4)
            b.wait("tensor", "sy", info["yrec"][ts])
            for bt in range(NBAT):
                js = list(range(bt * 4, min(KC, bt * 4 + 4)))
                gb = b.aux.get("tbat", 0)
                b.aux["tbat"] = gb + 1
                bank = gb % 8
                b.bank_wait("tensor", [bank])
                for jj, j in enumerate(js):
                    b.op("tensor", lambda E, j=j, jj=jj, bank=bank, ts=ts,
                         yring=yring: E.transpose(
                             psb[:, bank * 1024 + jj * 128:
                                 bank * 1024 + (jj + 1) * 128],
                             ybf(ts, yring)[:, j * 128:(j + 1) * 128],
                             idnb_ap),
                         incs=([("ptr", 1)] if j == js[-1] else []))
                b.wait("vector", "ptr", b.n("ptr"))
                nj = len(js)
                src = psb[:, bank * 1024:bank * 1024 + nj * 128].rearrange(
                    "p (j q) -> p j q", q=128)
                dst = yt3[:, bt * 4:bt * 4 + nj, ts * 128:ts * 128 + 128]
                b.op("vector", lambda E, src=src, dst=dst: E.tensor_copy(
                    dst, src), incs=[("vte", 1)])
                b.bank_set([bank], "vte", b.n("vte"))

        def _ln_entry_waits(extra_waits):
            for eng, sname, val in extra_waits:
                b.wait(eng, sname, val)
            prevv = b.last_done.get("vector")
            if prevv is not None:
                b.wait("scalar", prevv[0], prevv[1])
            for sl in range(3):
                b.wait("scalar", f"gac{sl}", b.n(f"gac{sl}"))
            for sl in range(2):
                b.wait("scalar", f"aac{sl}", b.n(f"aac{sl}"))

        def _ln_close(base):
            b.aux["ln_prev"] = dict(sy=b.n("sy"), vrs=b.n("vrs"),
                                    ptr=base["ptr"] + NBAT * NSUB)

        def ln_junction_items(tile, yring, extra_waits=()):
            state = {}

            def start():
                b.aux["yring"] = yring
                state["prevln"] = b.aux.get("ln_prev")
                state["base"] = {"sy": b.n("sy"), "vrs": b.n("vrs"),
                                 "ptr": b.n("ptr")}
                _ln_entry_waits(extra_waits)
                state["info"] = dict(tile=tile, yrec=[], yring=yring,
                                     ptr_base=state["base"]["ptr"])

            items = [start]
            nfirst = min(yring, NSUB)

            def mk_stats(ts):
                def f():
                    state["info"]["yrec"].append(
                        _stats_ts(tile, ts, state["base"], state["prevln"]))
                return f

            def mk_transp(ts):
                def f():
                    _transp_ts(state["info"], ts)
                return f

            for ts in range(nfirst):
                items.append(mk_stats(ts))
            for ts in range(nfirst, NSUB):
                items.append(mk_transp(ts - nfirst))
                items.append(mk_stats(ts))

            def finish():
                state["info"]["done"] = NSUB - nfirst
                _ln_close(state["base"])
                b.aux["pending_ln"] = state["info"]
                b.aux["yring"] = 4

            items.append(finish)
            return items

        def ln_transposes(info):
            for ts in range(info.get("done", 0), NSUB):
                _transp_ts(info, ts)
            return b.n("vte")

        def ln_full(tile, extra_waits=()):
            prevln = b.aux.get("ln_prev")
            base = {"sy": b.n("sy"), "vrs": b.n("vrs"), "ptr": b.n("ptr")}
            _ln_entry_waits(extra_waits)
            info = dict(tile=tile, yrec=[], ptr_base=base["ptr"])
            for ts in range(NSUB):
                info["yrec"].append(_stats_ts(tile, ts, base, prevln))
                _transp_ts(info, ts)
            _ln_close(base)
            return b.n("vte")

        def ln_junction(tile, yring=4):
            def j():
                extra = [("scalar", "vhh", b.n("vhh"))]
                return ln_junction_items(tile, yring, extra_waits=extra)
            return j

        def take_pending_ln(tile):
            pend = b.aux.pop("pending_ln", None)
            if pend is not None:
                assert pend["tile"] == tile
                return ln_transposes(pend)
            return None

        def ffn_phase(tile, is_int, junction=None):
            b.mark("ln")
            b.wait("vector", "bld", b.n("bld"))
            yT_ready = take_pending_ln(tile)
            if yT_ready is None:
                yT_ready = ln_full(tile)
            ln_ptr_final = b.n("ptr")
            if is_int:
                b.aux["f1a_ptr"] = ln_ptr_final

            for half in range(2):
                b.mark(f"gu{half}")
                hh_base = b.n("vhh")
                flush_base_prev_half = b.n("vfl")
                for ic in range(ICH):
                    icg = half * ICH + ic
                    par = ic % 2
                    pg = ps[:, par * 2048:par * 2048 + TILE]
                    pu = ps[:, par * 2048 + 1024:par * 2048 + 1024 + TILE]
                    for stream in ("g", "u"):
                        dint = d_gwt if stream == "g" else d_uwt
                        dscr = d_gwb if stream == "g" else d_uwb
                        wring = wg if stream == "g" else wu
                        nslab = b.n(stream + "slab")
                        b.cnt[stream + "slab"] = nslab + 1
                        slot = nslab % 2
                        soff = slot * KC * 128
                        pe_done = (lambda n: 2 * n + 1) if stream == "g" else \
                            (lambda n: 2 * n + 2)
                        wbrec = b.aux.setdefault(stream + "wbrec", {})
                        if is_int:
                            ioff = (IGOFF if stream == "g" else IUOFF) + slot * D
                            if nslab >= 2:
                                b.wait("sync", stream + "dq", nslab - 1)
                            b.op("sync", lambda E, dint=dint, icg=icg, ioff=ioff:
                                 E.dma_start(out=arbi[:, ioff:ioff + D],
                                             in_=dint[icg]),
                                 incs=[(f"{stream}il{slot}", 16)], dma=True)
                            if nslab >= 2:
                                b.wait("scalar", "pgu", pe_done(nslab - 2))
                                wname, wval = wbrec[nslab - 2]
                                b.wait("scalar", wname, wval)
                            b.wait("scalar", f"{stream}il{slot}",
                                   b.n(f"{stream}il{slot}"))
                            b.op("scalar", lambda E, ioff=ioff, wring=wring,
                                 soff=soff: E.activation(
                                     out=wring[:, soff:soff + KC * 128],
                                     in_=arbi[:, ioff:ioff + D],
                                     func=AF.Copy, bias=-128.0),
                                 incs=[(stream + "dq", 1)])
                            b.wait("gpsimd", stream + "dq", nslab + 1)
                            b.op("gpsimd", lambda E, dscr=dscr, icg=icg,
                                 wring=wring, soff=soff: E.dma_start(
                                     out=dscr[icg],
                                     in_=wring[:, soff:soff + KC * 128]),
                                 incs=[(f"{stream}wb{slot}", 16)], dma=True)
                            wbrec[nslab] = (f"{stream}wb{slot}",
                                            b.n(f"{stream}wb{slot}"))
                            b.wait("tensor", stream + "dq", nslab + 1)
                        else:
                            if nslab >= 2:
                                b.wait("sync", "pgu", pe_done(nslab - 2))
                            b.op("sync", lambda E, dscr=dscr, icg=icg,
                                 wring=wring, soff=soff: E.dma_start(
                                     out=wring[:, soff:soff + KC * 128],
                                     in_=dscr[icg]),
                                 incs=[(f"{stream}ld{slot}", 16)], dma=True)
                            b.wait("tensor", f"{stream}ld{slot}",
                                   b.n(f"{stream}ld{slot}"))
                        dst = pg if stream == "g" else pu
                        if stream == "g":
                            if icg == 0:
                                b.wait("tensor", "vte", yT_ready)
                            b.bank_wait("tensor",
                                        range(par * 4, par * 4 + 2 * NMMT))
                        for k in range(KC):
                            for j in range(NMMT):
                                w = min(512, TILE - j * 512)
                                last = (k == KC - 1 and j == NMMT - 1)
                                b.op("tensor", lambda E, k=k, j=j, w=w, dst=dst,
                                     wring=wring, soff=soff: E.matmul(
                                         dst[:, j * 512:j * 512 + w],
                                         wring[:, soff + k * 128:
                                               soff + (k + 1) * 128],
                                         yt[:, k * TILE + j * 512:
                                            k * TILE + j * 512 + w],
                                         start=(k == 0), stop=(k == KC - 1)),
                                     incs=([("pgu", 1)] if last else []))
                    sg = arf[:, Y0 + par * TILE:Y0 + (par + 1) * TILE]
                    if ic < 2:
                        b.wait("scalar", "ptr", ln_ptr_final)
                    else:
                        b.wait("scalar", "vhh", hh_base + ic - 1)
                    b.wait("scalar", "pgu", b.n("pgu") - 1)
                    b.op("scalar", lambda E, pg=pg, sg=sg, icg=icg: E.activation(
                        out=sg, in_=pg, func=AF.Silu, scale=gs_ap(icg)),
                        incs=[("ssi", 1)])
                    b.bank_set(range(par * 4, par * 4 + NMMT), "ssi",
                               b.n("ssi"))
                    b.wait("vector", "pgu", b.n("pgu"))
                    b.wait("vector", "ssi", b.n("ssi"))
                    if ic == 0:
                        b.wait("vector", "dpe", b.aux.get("half_dpe", 0))
                        b.wait("vector", "pq", b.n("pq"))
                    b.op("vector", lambda E, sg=sg, pu=pu, ic=ic, icg=icg:
                         E.scalar_tensor_tensor(
                             out=arb[:, ic * TILE:(ic + 1) * TILE],
                             in0=sg, scalar=us_ap(icg), in1=pu,
                             op0=ALU.mult, op1=ALU.mult),
                         incs=[("vhh", 1)])
                    b.bank_set(range(par * 4 + NMMT, par * 4 + 2 * NMMT),
                               "vhh", b.n("vhh"))
                hh_done = b.n("vhh")

                trickle = []
                if half == 1 and junction is not None:
                    b.mark("junc")
                    r = junction()
                    if isinstance(r, list):
                        trickle = r

                b.mark(f"dn{half}")
                ntq = (NSUB + 3) // 4
                nch_per = max(1, ICH // 2)
                if is_int:
                    b.wait("sync", "ptr", b.aux.get("f1a_ptr", 0))
                dstart_chunk = b.n("dch")
                for tq in range(ntq):
                    tss = list(range(tq * 4, min(NSUB, tq * 4 + 4)))
                    for hq in range(4):
                        gidx = b.aux.get("dgrp", 0)
                        b.aux["dgrp"] = gidx + 1
                        bbase = 4 * (gidx % 2)
                        grp_pdn_base = b.n("pdn")
                        for ch in range(nch_per):
                            ics = ([2 * ch, 2 * ch + 1] if ICH >= 2 else [0])
                            nsl = len(ics)
                            nch = b.n("dch")
                            b.cnt["dch"] = nch + 1
                            slot = nch % 4
                            soff = slot * 2 * D4
                            src_i = d_dwt[half, hq].rearrange(
                                "(cc p) h -> p cc h", p=128)[
                                :, ics[0]:ics[0] + nsl, :]
                            src_b = d_dwb[half, hq].rearrange(
                                "(cc p) h -> p cc h", p=128)[
                                :, ics[0]:ics[0] + nsl, :]
                            dstw = wd[:, soff:soff + nsl * D4].rearrange(
                                "p (cc h) -> p cc h", h=D4)
                            chunkwb = b.aux.setdefault("dch_wb", {})
                            if is_int and tq == 0:
                                idch = b.n("idch")
                                b.cnt["idch"] = idch + 1
                                islot = idch % 2
                                ioff = Y1 + islot * 2 * D4
                                if idch >= 2:
                                    b.wait("sync", "ddq", idch - 1)
                                b.op("sync", lambda E, src_i=src_i, ioff=ioff,
                                     nsl=nsl: E.dma_start(
                                         out=arfi[:, ioff:ioff + nsl * D4]
                                         .rearrange("p (cc h) -> p cc h", h=D4),
                                         in_=src_i),
                                     incs=[(f"dil{islot}", 16)], dma=True)
                                if nch >= 4:
                                    b.wait("scalar", "dpe", nch - 3)
                                    wbprev = chunkwb.get(nch - 4)
                                    if wbprev is not None:
                                        b.wait("scalar", wbprev[0], wbprev[1])
                                b.wait("scalar", f"dil{islot}",
                                       b.n(f"dil{islot}"))
                                b.op("scalar", lambda E, ioff=ioff, soff=soff,
                                     nsl=nsl: E.activation(
                                         out=wd[:, soff:soff + nsl * D4],
                                         in_=arfi[:, ioff:ioff + nsl * D4],
                                         func=AF.Copy, bias=-128.0),
                                     incs=[("ddq", 1)])
                                b.wait("gpsimd", "ddq", b.n("ddq"))
                                b.op("gpsimd", lambda E, src_b=src_b, dstw=dstw:
                                     E.dma_start(out=src_b, in_=dstw),
                                     incs=[(f"dwb{slot}", 16)], dma=True)
                                chunkwb[nch] = (f"dwb{slot}", b.n(f"dwb{slot}"))
                                b.wait("tensor", "ddq", b.n("ddq"))
                            else:
                                if is_int and tq == 1 and hq == 0 and ch == 0:
                                    for sl in range(4):
                                        b.wait("sync", f"dwb{sl}",
                                               b.n(f"dwb{sl}"))
                                if nch >= 4:
                                    b.wait("sync", "dpe", nch - 3)
                                    wbprev = chunkwb.get(nch - 4)
                                    if wbprev is not None:
                                        b.wait("sync", wbprev[0], wbprev[1])
                                chunkwb[nch] = None
                                b.op("sync", lambda E, src_b=src_b, dstw=dstw:
                                     E.dma_start(out=dstw, in_=src_b),
                                     incs=[(f"dld{slot}", 16)], dma=True)
                                b.wait("tensor", f"dld{slot}",
                                       b.n(f"dld{slot}"))
                            if nch == dstart_chunk:
                                b.wait("tensor", "vhh", hh_done)
                            for jj, ic in enumerate(ics):
                                for tsi, tsv in enumerate(tss):
                                    bk = bbase + tsi
                                    psd = ps[:, bk * 512:bk * 512 + D4]
                                    if ic == 0:
                                        b.bank_wait("tensor", [bk])
                                    lastmm = (ic == ICH - 1)
                                    lastofchunk = (jj == nsl - 1 and
                                                   tsi == len(tss) - 1)
                                    incs = []
                                    if lastofchunk:
                                        incs.append(("dpe", 1))
                                    elif lastmm and tsi < len(tss) - 1:
                                        incs.append(("pdn", 1))
                                    b.op("tensor", lambda E, psd=psd, soff=soff,
                                         jj=jj, tsv=tsv, ic=ic: E.matmul(
                                             psd,
                                             arb[:, ic * TILE + tsv * 128:
                                                 ic * TILE + tsv * 128 + 128],
                                             wd[:, soff + jj * D4:
                                                soff + (jj + 1) * D4],
                                             start=(ic == 0),
                                             stop=(ic == ICH - 1)),
                                         incs=incs)
                        for tsi, tsv in enumerate(tss):
                            bk = bbase + tsi
                            psd = ps[:, bk * 512:bk * 512 + D4]
                            nfl = b.n("vfl")
                            tslot = nfl % 3
                            toff = TMP + tslot * TILE
                            if tsi < len(tss) - 1:
                                b.wait("vector", "pdn", grp_pdn_base + tsi + 1)
                            else:
                                b.wait("vector", "dpe", b.n("dch"))
                            b.wait("vector", f"gac{tslot}",
                                   b.n(f"gac{tslot}"))
                            b.op("vector", lambda E, psd=psd, toff=toff,
                                 hq=hq: E.tensor_tensor(
                                     out=arf[:, toff:toff + D4], in0=psd,
                                     in1=arf[:, DSBC + hq * D4:
                                             DSBC + hq * D4 + D4],
                                     op=ALU.mult),
                                 incs=[("vfl", 1)])
                            b.bank_set([bk], "vfl", b.n("vfl"))
                            r0 = tile * TILE + tsv * 128
                            b.wait("gpsimd", "vfl", b.n("vfl"))
                            accrec = b.aux.setdefault("accrec", {})
                            prevacc = accrec.get((r0, hq))
                            if prevacc is not None:
                                b.wait("gpsimd", prevacc[0], prevacc[1])
                            b.op("gpsimd", lambda E, toff=toff, r0=r0, hq=hq:
                                 E.dma_start(
                                     out=d_xo[r0:r0 + 128,
                                              hq * D4:hq * D4 + D4],
                                     in_=arf[:, toff:toff + D4],
                                     accum_op=ALU.add),
                                 incs=[(f"gac{tslot}", 16)], dma=True)
                            accrec[(r0, hq)] = (f"gac{tslot}",
                                                b.n(f"gac{tslot}"))
                        if trickle:
                            per = max(1, -(-len(trickle) // (ntq * 4)))
                            for _ in range(per):
                                if trickle:
                                    trickle.pop(0)()
                while trickle:
                    trickle.pop(0)()
                b.aux["half_dpe"] = b.n("dch")

        def wq_off(k):
            return WQ_OFFS[k]

        def preload_B():
            for s in ("gdq", "udq", "ddq"):
                b.wait("gpsimd", s, b.n(s))
            for s in ("gil0", "gil1", "uil0", "uil1", "dil0", "dil1"):
                b.wait("gpsimd", s, b.n(s))
            for k in range(KC):
                b.op("gpsimd", lambda E, k=k: E.dma_start(
                    out=arb[:, WKVB2 + k * KVW:WKVB2 + (k + 1) * KVW],
                    in_=d_wkv[k * 128:(k + 1) * 128, :]),
                    incs=[("qlb", 16)], dma=True)
            for k in range(KC):
                if WQ_OFFS[k] >= RB:
                    b.op("gpsimd", lambda E, k=k: E.dma_start(
                        out=arb[:, wq_off(k):wq_off(k) + D],
                        in_=d_wq[k * 128:(k + 1) * 128, :]),
                        incs=[("qlb", 16)], dma=True)

        def load_A():
            b.wait("gpsimd", "dpe", b.n("dpe"))
            b.wait("gpsimd", "vhh", b.n("vhh"))
            for k in range(KC):
                if WQ_OFFS[k] < RB:
                    b.op("gpsimd", lambda E, k=k: E.dma_start(
                        out=arb[:, wq_off(k):wq_off(k) + D],
                        in_=d_wq[k * 128:(k + 1) * 128, :]),
                        incs=[("qla", 16)], dma=True)

        def attn_phase(tile, first, junction=None):
            BKVB = TMP + D + KVW
            if first:
                load_A()
                b.wait("sync", "vfl", b.n("vfl"))
                b.wait("sync", "bst", b.n("bst"))
                b.op("sync", lambda E: E.dma_start(
                    out=arf[:, DSBC:DSBC + D], in_=d_bc[0][:, 0:D]),
                    incs=[("bld", 16)], dma=True)
                for sl in range(3):
                    b.wait("sync", f"gac{sl}", b.n(f"gac{sl}"))
                b.op("sync", lambda E: E.dma_start(
                    out=arf[:, BKVB:BKVB + KVW], in_=d_bc[2][:, 0:KVW]),
                    incs=[("bld", 16)], dma=True)
            b.mark("aln")
            yT_ready = take_pending_ln(tile)
            if yT_ready is None:
                yT_ready = ln_full(tile)

            b.mark("akv")
            kv_base = b.n("vkv")
            b.wait("vector", "dpe", b.n("dpe"))
            b.wait("vector", "bld", b.n("bld"))
            for ts in range(NSUB):
                par = ts % 2
                pkv = ps[:, par * 1024:par * 1024 + KVW]
                if ts == 0:
                    b.wait("tensor", "vte", yT_ready)
                    b.wait("tensor", "qlb", b.n("qlb"))
                b.bank_wait("tensor",
                            range(par * 2, par * 2 + (KVW + 511) // 512))
                for k in range(KC):
                    for j in range(NKV):
                        w = min(512, KVW - j * 512)
                        last = (k == KC - 1 and j == NKV - 1)
                        b.op("tensor", lambda E, k=k, j=j, w=w, pkv=pkv, ts=ts:
                             E.matmul(
                                 pkv[:, j * 512:j * 512 + w],
                                 yt3[:, k, ts * 128:ts * 128 + 128],
                                 arb[:, WKVB2 + k * KVW + j * 512:
                                     WKVB2 + k * KVW + j * 512 + w],
                                 start=(k == 0), stop=(k == KC - 1)),
                             incs=([("pkv", 1)] if last else []))
                b.wait("vector", "pkv", b.n("pkv"))
                b.op("vector", lambda E, pkv=pkv, ts=ts: E.tensor_add(
                    arbf[:, KVSF2 + ts * KVW:KVSF2 + (ts + 1) * KVW], pkv,
                    arf[:, BKVB:BKVB + KVW]), incs=[("vkv", 1)])
                b.bank_set(range(par * 2, par * 2 + (KVW + 511) // 512),
                           "vkv", b.n("vkv"))

            b.mark("aq")
            qb_base = b.n("vqb")
            b.wait("vector", "sy", b.n("sy"))
            for sl in range(2):
                b.wait("vector", f"aac{sl}", b.n(f"aac{sl}"))
            for ts in range(NSUB):
                par = ts % 2
                pq = ps[:, par * 2048:par * 2048 + D]
                if ts == 0:
                    b.wait("tensor", "qla", b.n("qla"))
                b.bank_wait("tensor",
                            range(par * 4, par * 4 + (D + 511) // 512))
                for k in range(KC):
                    for j in range(NQ):
                        w = min(512, D - j * 512)
                        last = (k == KC - 1 and j == NQ - 1)
                        b.op("tensor", lambda E, k=k, j=j, w=w, pq=pq, ts=ts:
                             E.matmul(
                                 pq[:, j * 512:j * 512 + w],
                                 yt3[:, k, ts * 128:ts * 128 + 128],
                                 arb[:, wq_off(k) + j * 512:
                                     wq_off(k) + j * 512 + w],
                                 start=(k == 0), stop=(k == KC - 1)),
                             incs=([("pq", 1)] if last else []))
                b.wait("vector", "pq", b.n("pq"))
                if ts >= 2:
                    b.wait("vector", "gpr", b.n("gpr") - 1)
                b.op("vector", lambda E, pq=pq: E.tensor_add(
                    arf[:, XS0:XS0 + D], pq, arf[:, DSBC:DSBC + D]),
                    incs=[("vqb", 1)])
                b.bank_set(range(par * 4, par * 4 + (D + 511) // 512),
                           "vqb", b.n("vqb"))
                kv0 = KVSF2 + ts * KVW
                kview = arbf[:, kv0:kv0 + KVW].rearrange(
                    "p (g c d) -> p g c d", g=G, c=2)[:, :, 0, :] \
                    .unsqueeze(2).broadcast_to([128, G, HPG, HD])
                qview = arf[:, XS0:XS0 + D].rearrange(
                    "p (g h d) -> p g h d", g=G, h=HPG)
                if ts == 0:
                    b.wait("gpsimd", "vkv", b.n("vkv"))
                b.wait("gpsimd", "vqb", b.n("vqb"))
                b.wait("gpsimd", "gsr", b.n("gsr"))
                b.op("gpsimd", lambda E, kview=kview, qview=qview:
                     E.tensor_tensor(
                         out=arf[:, XS1:XS1 + D].rearrange(
                             "p (g h d) -> p g h d", g=G, h=HPG),
                         in0=qview, in1=kview, op=ALU.mult),
                     incs=[("gpr", 1)])
                b.wait("vector", "gpr", b.n("gpr"))
                b.op("vector", lambda E, ts=ts: E.reduce_sum(
                    arf[:, TMP + ts * GH:TMP + (ts + 1) * GH],
                    arf[:, XS1:XS1 + D].rearrange("p (s d) -> p s d", d=HD),
                    axis=AX.X), incs=[("gsr", 1)])

            if junction is not None:
                b.mark("junc")
                junction()

            SALL = TMP
            S8 = TMP + NSUB * GH
            D0 = S8 + NSUB * GH
            EE = D0 + NSUB * GH
            ATT = EE + NSUB * GH
            nsc = NSUB * GH
            b.wait("vector", "gsr", b.n("gsr"))
            b.op("vector", lambda E: E.tensor_scalar_mul(
                arf[:, S8:S8 + nsc], arf[:, SALL:SALL + nsc], 1.0 / 8.0))
            b.op("vector", lambda E: E.tensor_sub(
                arf[:, D0:D0 + nsc], arf[:, S8:S8 + nsc],
                arf[:, S8:S8 + nsc]), incs=[("vd0", 1)])
            b.wait("scalar", "vd0", b.n("vd0"))
            b.op("scalar", lambda E: E.activation(
                out=arf[:, EE:EE + nsc], in_=arf[:, D0:D0 + nsc], func=AF.Exp),
                incs=[("sxp", 1)])
            b.wait("vector", "sxp", b.n("sxp"))
            b.op("vector", lambda E: E.reciprocal(
                arf[:, D0:D0 + nsc], arf[:, EE:EE + nsc]))
            b.op("vector", lambda E: E.tensor_mul(
                arf[:, ATT:ATT + nsc], arf[:, EE:EE + nsc],
                arf[:, D0:D0 + nsc]))
            b.mark("aoa")
            b.wait("vector", "sy", b.n("sy"))
            for ts in range(NSUB):
                oslot = XS1 if ts % 2 else XS0
                b.wait("vector", f"aac{ts % 2}", b.n(f"aac{ts % 2}"))
                kv0 = KVSF2 + ts * KVW
                aview = arf[:, ATT + ts * GH:ATT + (ts + 1) * GH].rearrange(
                    "p (g h) -> p g h", g=G).unsqueeze(3) \
                    .broadcast_to([128, G, HPG, HD])
                vview = arbf[:, kv0:kv0 + KVW].rearrange(
                    "p (g c d) -> p g c d", g=G, c=2)[:, :, 1, :] \
                    .unsqueeze(2).broadcast_to([128, G, HPG, HD])
                b.op("vector", lambda E, aview=aview, vview=vview, oslot=oslot:
                     E.tensor_tensor(
                         out=arf[:, oslot:oslot + D].rearrange(
                             "p (g h d) -> p g h d", g=G, h=HPG),
                         in0=aview, in1=vview, op=ALU.mult),
                     incs=[("voa", 1)])
                r0 = tile * TILE + ts * 128
                b.wait("gpsimd", "voa", b.n("voa"))
                accrec = b.aux.setdefault("accrec", {})
                for hq in range(4):
                    prevacc = accrec.get((r0, hq))
                    if prevacc is not None:
                        b.wait("gpsimd", prevacc[0], prevacc[1])
                b.op("gpsimd", lambda E, oslot=oslot, r0=r0, ts=ts:
                     E.dma_start(
                         out=d_xo[r0:r0 + 128, :], in_=arf[:, oslot:oslot + D],
                         accum_op=ALU.add), incs=[(f"aac{ts % 2}", 16)],
                     dma=True)
                for hq in range(4):
                    accrec[(r0, hq)] = (f"aac{ts % 2}", b.n(f"aac{ts % 2}"))

        b.wait("scalar", "xinit", 32)
        b.wait("scalar", "cld", b.n("cld"))
        b.wait("vector", "vbc", b.n("vbc"))
        b.mark("f1a")
        ffn_phase(0, is_int=True, junction=ln_junction(1, yring=2))
        for s in ("gwb0", "gwb1", "uwb0", "uwb1",
                  "dwb0", "dwb1", "dwb2", "dwb3"):
            b.wait("sync", s, b.n(s))
        b.mark("f1b")
        preload_B()
        ffn_phase(1, is_int=False, junction=ln_junction(0))
        b.mark("att0")
        attn_phase(0, first=True, junction=ln_junction(1))
        b.mark("att1")

        def f2a_junction():
            b.wait("sync", "vqb", b.n("vqb"))
            b.op("sync", lambda E: E.dma_start(
                out=arf[:, DSBC:DSBC + D], in_=d_bc[1][:, 0:D]),
                incs=[("bld", 16)], dma=True)
            ln_junction(0)()

        attn_phase(1, first=False, junction=f2a_junction)
        b.mark("f2a")
        ffn_phase(0, is_int=False, junction=ln_junction(1))
        b.mark("f2b")
        ffn_phase(1, is_int=False)
        b.mark("tail")
        for sl in range(3):
            b.wait("gpsimd", f"gac{sl}", b.n(f"gac{sl}"))
        for sl in range(2):
            b.wait("gpsimd", f"aac{sl}", b.n(f"aac{sl}"))

        segs = []
        prev = ("init", {e: 0 for e in ENGS})
        for name, pos in b.marks + [("end", {e: len(b.ops[e]) for e in ENGS})]:
            segs.append((prev[0], prev[1], pos))
            prev = (name, pos)
        segs.append((prev[0], prev[1], {e: len(b.ops[e]) for e in ENGS}))

        with nc.Block() as block:
            for eng in ENGS:
                def body(E, eng=eng):
                    idx = 0
                    for sname, p0, p1 in segs:
                        lo, hi = p0[eng], p1[eng]
                        if hi > lo:
                            with nc.named_scope(sname):
                                for fn in b.ops[eng][lo:hi]:
                                    fn(E)
                getattr(block, eng)(body)

    return nc



def kernel(L0_attn_x, L0_attn_Wq, L0_attn_bq, L0_attn_Wkv, L0_attn_bkv,
           L0_ffn_gate_w, L0_ffn_gate_scale, L0_ffn_up_w, L0_ffn_up_scale,
           L0_ffn_down_w, L0_ffn_down_scale):
    cfg = real_cfg()
    c = derive(cfg)
    n_cores = 8
    in_maps = host_prep(cfg, L0_attn_x, L0_attn_Wq, L0_attn_bq, L0_attn_Wkv,
                        L0_attn_bkv, L0_ffn_gate_w, L0_ffn_gate_scale,
                        L0_ffn_up_w, L0_ffn_up_scale, L0_ffn_down_w,
                        L0_ffn_down_scale, n_cores)
    nc = build_program(cfg)
    res = run_bass_kernel_spmd(nc, in_maps, core_ids=list(range(n_cores)))
    out = np.concatenate([np.asarray(r["xo"], dtype=np.float32)
                          for r in res.results], axis=0)
    return out.reshape(1, c["T"] * n_cores, c["D"])
